# version 03 (final)
# baseline (speedup 1.0000x reference)
from contextlib import ExitStack
import numpy as np
import concourse.bass as bass
import concourse.mybir as mybir

F32 = mybir.dt.float32
BF16 = mybir.dt.bfloat16
AF = mybir.ActivationFunctionType
ALU = mybir.AluOpType
AX = mybir.AxisListType

S = 4096
NT = S // 128
D = 1024
KT = D // 128



_SEM_ES = None


def sem(nc, name):
    return _SEM_ES.enter_context(nc.semaphore(name))

def cdiv(a, b):
    return (a + b - 1) // b


def build_consts(nc, es):
    ident_bf = es.enter_context(nc.sbuf_tensor("ident_bf", [128, 128], BF16))
    ident_f32 = es.enter_context(nc.sbuf_tensor("ident_f32", [128, 128], F32))
    causal_bias = es.enter_context(nc.sbuf_tensor("causal_bias", [128, 128], F32))
    zero_bf = es.enter_context(nc.sbuf_tensor("zero_bf", [128, 128], BF16))
    ones_row = es.enter_context(nc.sbuf_tensor("ones_row", [1, 512], F32))
    epsln = es.enter_context(nc.sbuf_tensor("epsln", [128, 1], F32))
    vd_scr = es.enter_context(nc.sbuf_tensor("vd_scr", [128, 128], F32))
    with nc.Block("consts") as blk:
        @blk.gpsimd
        def _(g):
            g.memset(ident_bf[:, :], 0.0)
            g.affine_select(ident_bf[:, :], ident_bf[:, :], [[-1, 128]],
                            ALU.not_equal, 1.0, base=0, channel_multiplier=1)
            g.memset(ident_f32[:, :], 0.0)
            g.affine_select(ident_f32[:, :], ident_f32[:, :], [[-1, 128]],
                            ALU.not_equal, 1.0, base=0, channel_multiplier=1)
            g.memset(causal_bias[:, :], 0.0)
            g.affine_select(causal_bias[:, :], causal_bias[:, :], [[-1, 128]],
                            ALU.is_ge, -1e30, base=0, channel_multiplier=1)
            g.memset(zero_bf[:, :], 0.0)
            g.memset(ones_row[:, :], 1.0)
            g.memset(epsln[:, :], 1e-5)
    nc.all_engine_barrier()
    return dict(ident_bf=ident_bf, ident_f32=ident_f32, causal_bias=causal_bias,
                zero_bf=zero_bf, ones_row=ones_row, epsln=epsln, vd_scr=vd_scr)


def vdelay(v, consts, reps=2):
    for _ in range(reps):
        v.tensor_copy(out=consts["vd_scr"][:, :], in_=consts["ident_f32"][:, :])


def cast_load(g, dst_sb, src_d, sem):
    rows = src_d.shape[0]
    ktn = rows // 128
    g.dma_start(
        out=dst_sb[:, :].rearrange("p (kt c) -> p kt c", kt=ktn),
        in_=src_d.rearrange("(kt p) c -> p kt c", p=128)).then_inc(sem, 16)
    return 16


def emit_ln_pass(nc, consts, src_rowtile_fn, n_tiles, out_lnT_sb,
                 name="ln", extra_row_fn=None, extra_row_dtype=F32,
                 out_lnT_dram=None):
    es = ExitStack()
    with es:
        xt = [es.enter_context(nc.sbuf_tensor(f"{name}_xt{i}", [128, D], F32)) for i in range(2)]
        xc = [es.enter_context(nc.sbuf_tensor(f"{name}_xc{i}", [128, D], F32)) for i in range(2)]
        lnrow = [es.enter_context(nc.sbuf_tensor(f"{name}_lr{i}", [128, D], BF16)) for i in range(2)]
        lnx = None
        if extra_row_fn is not None:
            lnx = [es.enter_context(nc.sbuf_tensor(f"{name}_lx{i}", [128, D], extra_row_dtype))
                   for i in range(2)]
        sq = es.enter_context(nc.sbuf_tensor(f"{name}_sq", [128, D], F32))
        sm = [es.enter_context(nc.sbuf_tensor(f"{name}_sm{i}", [128, 1], F32)) for i in range(8)]
        (sums, sumsq, m, m2, ex2, var, lnv) = sm[:7]
        rstd = es.enter_context(nc.sbuf_tensor(f"{name}_rstd", [128, 1], F32))
        negrm = es.enter_context(nc.sbuf_tensor(f"{name}_negrm", [128, 1], F32))
        psT = [es.enter_context(nc.psum_tensor(f"{name}_psT{i}", [128, 1024], BF16)) for i in range(2)]
        stage = None
        if out_lnT_sb is None:
            stage = [es.enter_context(nc.sbuf_tensor(f"{name}_stg{i}", [128, KT * 128], BF16))
                     for i in range(2)]
        sld = sem(nc, f"{name}_ld")
        sts = sem(nc, f"{name}_ts")
        sv = sem(nc, f"{name}_v")
        ss = sem(nc, f"{name}_s")
        stt = sem(nc, f"{name}_t")
        sst = sem(nc, f"{name}_st")
        NV, NS = 7, 4 + (1 if lnx is not None else 0)

        with nc.Block(name) as blk:
            @blk.gpsimd
            def _(g):
                for it in range(n_tiles):
                    if it >= 2:
                        g.wait_ge(ss, (it - 2) * NS + NS)
                    g.dma_start(out=xt[it % 2][:, :], in_=src_rowtile_fn(it)).then_inc(sld, 16)
                    if it >= 1:
                        pit = it - 1
                        if lnx is not None:
                            g.wait_ge(ss, pit * NS + 5)
                            g.dma_start(out=extra_row_fn(pit), in_=lnx[pit % 2][:, :]).then_inc(sst, 16)
                        if out_lnT_sb is None:
                            g.wait_ge(sv, pit * NV + 7)
                            g.dma_start(
                                out=out_lnT_dram.rearrange("(kt p) s -> p kt s", p=128)[:, :, pit * 128:(pit + 1) * 128],
                                in_=stage[pit % 2][:, :].rearrange("p (kt i) -> p kt i", kt=KT)
                            ).then_inc(sts, 16)
                pit = n_tiles - 1
                if lnx is not None:
                    g.wait_ge(ss, pit * NS + 5)
                    g.dma_start(out=extra_row_fn(pit), in_=lnx[pit % 2][:, :]).then_inc(sst, 16)
                    g.wait_ge(sst, 16 * n_tiles)
                if out_lnT_sb is None:
                    g.wait_ge(sv, pit * NV + 7)
                    g.dma_start(
                        out=out_lnT_dram.rearrange("(kt p) s -> p kt s", p=128)[:, :, pit * 128:(pit + 1) * 128],
                        in_=stage[pit % 2][:, :].rearrange("p (kt i) -> p kt i", kt=KT)
                    ).then_inc(sts, 16)
                    g.wait_ge(sts, 16 * n_tiles)

            @blk.vector
            def _(v):
                for it in range(n_tiles):
                    b = it % 2
                    v.wait_ge(sld, 16 * (it + 1))
                    v.reduce_sum(sums[:, :], xt[b][:, :], axis=AX.X).then_inc(sv, 1)
                    v.wait_ge(sv, it * NV + 1)
                    v.tensor_scalar_mul(m[:, :], sums[:, :], 1.0 / D).then_inc(sv, 1)
                    v.wait_ge(sv, it * NV + 2)
                    v.tensor_tensor(m2[:, :], m[:, :], m[:, :], ALU.mult).then_inc(sv, 1)
                    v.wait_ge(ss, it * NS + 1)
                    v.tensor_scalar_mul(ex2[:, :], sumsq[:, :], 1.0 / D).then_inc(sv, 1)
                    v.wait_ge(sv, it * NV + 4)
                    v.tensor_tensor(var[:, :], ex2[:, :], m2[:, :], ALU.subtract).then_inc(sv, 1)
                    if it >= 2:
                        v.wait_ge(ss, (it - 2) * NS + NS)
                    v.tensor_scalar(xc[it % 2][:, :], xt[it % 2][:, :], m[:, :], None,
                                    ALU.subtract).then_inc(sv, 1)
                    v.wait_ge(stt, it + 1)
                    if out_lnT_sb is not None:
                        v.tensor_copy(
                            out=out_lnT_sb[:, :].rearrange("p (kt i) -> p kt i", kt=KT)[:, :, it * 128:(it + 1) * 128],
                            in_=psT[b][:, :].rearrange("p (kt i) -> p kt i", kt=KT)).then_inc(sv, 1)
                    else:
                        if it >= 2:
                            v.wait_ge(sts, 16 * (it - 1))
                        v.tensor_copy(out=stage[b][:, :], in_=psT[b][:, :]).then_inc(sv, 1)

            @blk.scalar
            def _(s):
                for it in range(n_tiles):
                    b = it % 2
                    s.wait_ge(sld, 16 * (it + 1))
                    s.activation(out=sq[:, :], in_=xt[b][:, :], func=AF.Square,
                                 accum_out=sumsq[:, :]).then_inc(ss, 1)
                    s.wait_ge(sv, it * NV + 5)
                    s.activation(out=lnv[:, :], in_=var[:, :], func=AF.Ln,
                                 bias=consts["epsln"][:, :], scale=1.0).then_inc(ss, 1)
                    s.wait_ge(ss, it * NS + 2)
                    s.activation(out=rstd[:, :], in_=lnv[:, :], func=AF.Exp,
                                 scale=-0.5).then_inc(ss, 1)
                    s.wait_ge(sv, it * NV + 6)
                    s.wait_ge(ss, it * NS + 3)
                    if it >= 2:
                        s.wait_ge(stt, it - 1)
                    s.activation(out=lnrow[b][:, :], in_=xc[b][:, :], func=AF.Copy,
                                 scale=rstd[:, :]).then_inc(ss, 1)
                    if lnx is not None:
                        if it >= 2:
                            s.wait_ge(sst, 16 * (it - 1))
                        s.activation(out=lnx[b][:, :], in_=xc[b][:, :], func=AF.Copy,
                                     scale=rstd[:, :]).then_inc(ss, 1)

            @blk.tensor
            def _(t):
                for it in range(n_tiles):
                    b = it % 2
                    t.wait_ge(ss, it * NS + 4)
                    if it >= 2:
                        t.wait_ge(sv, (it - 2) * NV + 7)
                    for kt in range(KT):
                        tr = t.transpose(psT[b][:, kt * 128:(kt + 1) * 128],
                                         lnrow[b][:, kt * 128:(kt + 1) * 128],
                                         consts["ident_bf"][:, :])
                    tr.then_inc(stt, 1)

        nc.all_engine_barrier()


def build_phase_a(nc, es, d, consts):
    SCALE = 0.125

    lnT = es.enter_context(nc.sbuf_tensor("a_lnT", [128, KT * S], BF16))
    vrow = es.enter_context(nc.sbuf_tensor("a_vrow", [128, NT * D], BF16))

    emit_ln_pass(nc, consts, lambda it: d["x0"][it * 128:(it + 1) * 128, :],
                 NT, lnT, name="a_ln")

    with ExitStack() as les:
        wv_sb = les.enter_context(nc.sbuf_tensor("a_wv", [128, KT * D], BF16))
        swv = sem(nc, "a_swv")
        with nc.Block("a_wvload") as blk:
            @blk.gpsimd
            def _(g):
                cast_load(g, wv_sb, d["wkv"][:, D:2 * D], swv)
                g.wait_ge(swv, 16)
        nc.all_engine_barrier()
        psV = [les.enter_context(nc.psum_tensor(f"a_psV{i}", [128, 512], F32)) for i in range(2)]
        smm = sem(nc, "a_v_mm")
        sev = sem(nc, "a_v_ev")
        with nc.Block("a_v") as blk:
            @blk.tensor
            def _(t):
                n = 0
                for it in range(NT):
                    for half in range(2):
                        if n >= 2:
                            t.wait_ge(sev, n - 1)
                        for kt in range(KT):
                            mm = t.matmul(psV[n % 2][:, :],
                                          lnT[:, kt * S + it * 128: kt * S + (it + 1) * 128],
                                          wv_sb[:, kt * D + half * 512: kt * D + (half + 1) * 512],
                                          start=(kt == 0), stop=(kt == KT - 1))
                        mm.then_inc(smm, 1)
                        n += 1

            @blk.vector
            def _(v):
                n = 0
                for it in range(NT):
                    for half in range(2):
                        v.wait_ge(smm, n + 1)
                        v.tensor_copy(out=vrow[:, it * D + half * 512: it * D + (half + 1) * 512],
                                      in_=psV[n % 2][:, :]).then_inc(sev, 1)
                        n += 1
        nc.all_engine_barrier()

    with ExitStack() as les:
        wq_p = les.enter_context(nc.sbuf_tensor("a_wq_p", [128, KT * 128], BF16))
        wk_p = les.enter_context(nc.sbuf_tensor("a_wk_p", [128, KT * 128], BF16))
        q2T = les.enter_context(nc.sbuf_tensor("a_q2T", [128, S], BF16))
        k2T = les.enter_context(nc.sbuf_tensor("a_k2T", [128, S], BF16))
        scores = les.enter_context(nc.sbuf_tensor("a_scores", [128, S], F32))
        E4 = les.enter_context(nc.sbuf_tensor("a_E4", [128, 2 * S], BF16))
        ET = [les.enter_context(nc.sbuf_tensor(f"a_ET{i}", [128, 256], BF16)) for i in range(2)]
        mrow = les.enter_context(nc.sbuf_tensor("a_mrow", [128, 1], F32))
        mneg = les.enter_context(nc.sbuf_tensor("a_mneg", [128, 1], F32))
        zrow = les.enter_context(nc.sbuf_tensor("a_zrow", [128, 1], F32))
        zinv = les.enter_context(nc.sbuf_tensor("a_zinv", [128, 1], F32))
        avout = les.enter_context(nc.sbuf_tensor("a_avout", [128, 256], BF16))
        psQ = [les.enter_context(nc.psum_tensor(f"a_psQ{i}", [128, 512], F32)) for i in range(2)]
        psS = [les.enter_context(nc.psum_tensor(f"a_psS{i}", [128, 512], F32)) for i in range(2)]
        psET = [les.enter_context(nc.psum_tensor(f"a_psET{i}", [128, 1024], BF16)) for i in range(2)]
        psA = les.enter_context(nc.psum_tensor("a_psA", [128, 512], F32))
        sems = {nm: sem(nc, f"a_{nm}")
                for nm in ["pqw", "pq_mm", "pq_ev", "mm", "sc", "v", "e", "tt", "te", "avj", "ao", "st"]}
        base = {nm: 0 for nm in sems}

        for hp in range(8):
            base = _emit_pair_proj(nc, d, hp, lnT, wq_p, wk_p, q2T, k2T, psQ, sems, base)
            for ic in range(16):
                base = _emit_attn_chunk(nc, consts, d, hp, ic, q2T, k2T, vrow,
                                        scores, E4, ET, mrow, mneg, zrow, zinv,
                                        avout, psS, psET, psA, sems, base)

    with ExitStack() as les:
        wo_sb = les.enter_context(nc.sbuf_tensor("a_wo", [128, KT * D], BF16))
        bo_bc = les.enter_context(nc.sbuf_tensor("a_bo_bc", [128, D], F32))
        bo_row = les.enter_context(nc.sbuf_tensor("a_bo_row", [1, D], F32))
        atT = [les.enter_context(nc.sbuf_tensor(f"a_atT{i}", [128, KT * 128], BF16)) for i in range(2)]
        x0t = [les.enter_context(nc.sbuf_tensor(f"a_x0t{i}", [128, D], F32)) for i in range(2)]
        x1t = [les.enter_context(nc.sbuf_tensor(f"a_x1t{i}", [128, D], F32)) for i in range(2)]
        psX = [les.enter_context(nc.psum_tensor(f"a_psX{i}", [128, 512], F32)) for i in range(2)]
        sld = sem(nc, "a_x1_ld")
        smm = sem(nc, "a_x1_mm")
        sev = sem(nc, "a_x1_ev")
        sst = sem(nc, "a_x1_st")
        svv = sem(nc, "a_x1_vv")

        with nc.Block("a_wo_load") as blk:
            @blk.gpsimd
            def _(g):
                cast_load(g, wo_sb, d["wo"], sld)
                g.dma_start(out=bo_row[:, :], in_=d["bo"][None, :]).then_inc(sld, 16)
                g.wait_ge(sld, 32)

            @blk.tensor
            def _(t):
                t.wait_ge(sld, 32)
                for half in range(2):
                    t.matmul(psX[half][:, :], consts["ones_row"][0:1, 0:128],
                             bo_row[0:1, half * 512:(half + 1) * 512],
                             start=True, stop=True).then_inc(smm, 1)

            @blk.vector
            def _(v):
                for half in range(2):
                    v.wait_ge(smm, half + 1)
                    v.tensor_copy(out=bo_bc[:, half * 512:(half + 1) * 512], in_=psX[half][:, :])
        nc.all_engine_barrier()

        with nc.Block("a_x1") as blk:
            @blk.gpsimd
            def _(g):
                for it in range(NT):
                    b = it % 2
                    if it >= 2:
                        g.wait_ge(smm, 2 + (it - 2) * 2 + 2)
                        g.wait_ge(sev, (it - 2) * 2 + 2)
                    g.dma_start(out=atT[b][:, :].rearrange("p (kt i) -> p kt i", kt=KT),
                                in_=d["attnT"].rearrange("(kt p) s -> p kt s", p=128)[:, :, it * 128:(it + 1) * 128]
                                ).then_inc(sld, 16)
                    g.dma_start(out=x0t[b][:, :],
                                in_=d["x0"][it * 128:(it + 1) * 128, :]).then_inc(sld, 16)
                    if it >= 1:
                        g.wait_ge(sev, (it - 1) * 2 + 2)
                        g.dma_start(out=d["x1"][(it - 1) * 128:it * 128, :],
                                    in_=x1t[(it - 1) % 2][:, :]).then_inc(sst, 16)
                g.wait_ge(sev, NT * 2)
                g.dma_start(out=d["x1"][(NT - 1) * 128:NT * 128, :],
                            in_=x1t[(NT - 1) % 2][:, :]).then_inc(sst, 16)
                g.wait_ge(sst, 16 * NT)

            @blk.tensor
            def _(t):
                for it in range(NT):
                    b = it % 2
                    t.wait_ge(sld, 32 + 32 * (it + 1))
                    for half in range(2):
                        if it >= 1:
                            t.wait_ge(sev, (it - 1) * 2 + half + 1)
                        for kt in range(KT):
                            mm = t.matmul(psX[half][:, :],
                                          atT[b][:, kt * 128:(kt + 1) * 128],
                                          wo_sb[:, kt * D + half * 512: kt * D + (half + 1) * 512],
                                          start=(kt == 0), stop=(kt == KT - 1))
                        mm.then_inc(smm, 1)

            @blk.vector
            def _(v):
                for it in range(NT):
                    b = it % 2
                    for half in range(2):
                        v.wait_ge(smm, 2 + it * 2 + half + 1)
                        if it >= 2:
                            v.wait_ge(sst, 16 * (it - 1))
                        sl = slice(half * 512, (half + 1) * 512)
                        v.tensor_tensor(x1t[b][:, sl], psX[half][:, :], x0t[b][:, sl],
                                        ALU.add).then_inc(svv, 1)
                        v.wait_ge(svv, it * 2 + half + 1)
                        v.tensor_tensor(x1t[b][:, sl], x1t[b][:, sl], bo_bc[:, sl], ALU.add).then_inc(sev, 1)

        nc.all_engine_barrier()


def _emit_pair_proj(nc, d, hp, lnT, wq_p, wk_p, q2T, k2T, psQ, sems, base):
    with nc.Block(f"a_qk{hp}") as blk:
        @blk.gpsimd
        def _(g):
            cast_load(g, wq_p, d["wq"][:, hp * 128:(hp + 1) * 128], sems["pqw"])
            cast_load(g, wk_p, d["wkv"][:, hp * 128:(hp + 1) * 128], sems["pqw"])

        @blk.tensor
        def _(t):
            t.wait_ge(sems["pqw"], base["pqw"] + 32)
            n = 0
            for which, wsb in ((0, wq_p), (1, wk_p)):
                for ic in range(8):
                    if n >= 2:
                        t.wait_ge(sems["pq_ev"], base["pq_ev"] + n - 1)
                    for kt in range(KT):
                        mm = t.matmul(psQ[n % 2][:, :],
                                      wsb[:, kt * 128:(kt + 1) * 128],
                                      lnT[:, kt * S + ic * 512: kt * S + (ic + 1) * 512],
                                      start=(kt == 0), stop=(kt == KT - 1))
                    mm.then_inc(sems["pq_mm"], 1)
                    n += 1

        @blk.vector
        def _(v):
            n = 0
            for which in range(2):
                dst = q2T if which == 0 else k2T
                for ic in range(8):
                    v.wait_ge(sems["pq_mm"], base["pq_mm"] + n + 1)
                    v.tensor_copy(out=dst[:, ic * 512:(ic + 1) * 512],
                                  in_=psQ[n % 2][:, :]).then_inc(sems["pq_ev"], 1)
                    n += 1
    nc.all_engine_barrier()
    base = dict(base)
    base["pq_mm"] += 16
    base["pq_ev"] += 16
    base["pqw"] += 32
    return base


def _emit_attn_chunk(nc, consts, d, hp, ic, q2T, k2T, vrow, scores, E4, ET,
                     mrow, mneg, zrow, zinv, avout, psS, psET, psA, sems, base):
    SCALE = 0.125
    CT = 2
    CW = CT * 128
    jmax = CT * ic + CT

    def nch(t4):
        return cdiv((CT * ic + t4 + 1) * 128, 512)

    NVT = 5

    with nc.Block(f"a_at{hp}_{ic}") as blk:
        @blk.tensor
        def _(t):
            mmn = 0
            ttn = 0
            for hh in range(2):
                hbase = 64 * hh
                for t4 in range(CT):
                    it = CT * ic + t4
                    for c in range(nch(t4)):
                        if mmn >= 2:
                            t.wait_ge(sems["sc"], base["sc"] + mmn - 1)
                        t.matmul(psS[mmn % 2][:, :],
                                 q2T[hbase:hbase + 64, it * 128:(it + 1) * 128],
                                 k2T[hbase:hbase + 64, c * 512:(c + 1) * 512],
                                 start=True, stop=True).then_inc(sems["mm"], 1)
                        mmn += 1
                for jb in range(jmax):
                    bt = ttn % 2
                    if ttn >= 2:
                        t.wait_ge(sems["te"], base["te"] + ttn - 1)
                    for t4 in range(CT):
                        it = CT * ic + t4
                        if jb <= it:
                            t.wait_ge(sems["v"], base["v"] + (hh * CT + t4) * NVT + 5)
                            src = E4[:, t4 * S + jb * 128: t4 * S + (jb + 1) * 128]
                        else:
                            src = consts["zero_bf"][:, :]
                        tr = t.transpose(psET[bt][:, t4 * 128:(t4 + 1) * 128], src,
                                         consts["ident_bf"][:, :])
                    tr.then_inc(sems["tt"], 1)
                    ttn += 1
                    t.wait_ge(sems["te"], base["te"] + ttn)
                    t.matmul(psA[hbase:hbase + 64, 0:CW],
                             vrow[:, jb * D + (2 * hp + hh) * 64: jb * D + (2 * hp + hh) * 64 + 64],
                             ET[(ttn - 1) % 2][:, :],
                             start=(jb == 0), stop=(jb == jmax - 1)).then_inc(sems["avj"], 1)

        @blk.vector
        def _(v):
            mmn = 0
            vn = 0
            ttn = 0
            for hh in range(2):
                for t4 in range(CT):
                    n = hh * CT + t4
                    it = CT * ic + t4
                    jlen = (it + 1) * 128
                    v.wait_ge(sems["e"], base["e"] + n)
                    for c in range(nch(t4)):
                        v.wait_ge(sems["mm"], base["mm"] + mmn + 1)
                        cl = min(512, jlen - c * 512)
                        v.tensor_copy(out=scores[:, c * 512: c * 512 + cl],
                                      in_=psS[mmn % 2][:, 0:cl]).then_inc(sems["sc"], 1)
                        mmn += 1
                    v.wait_ge(sems["sc"], base["sc"] + mmn)
                    v.tensor_tensor(scores[:, jlen - 128: jlen],
                                    scores[:, jlen - 128: jlen],
                                    consts["causal_bias"][:, :], ALU.add).then_inc(sems["v"], 1)
                    v.wait_ge(sems["v"], base["v"] + vn + 1)
                    v.reduce_max(mrow[:, :], scores[:, 0:jlen], axis=AX.X).then_inc(sems["v"], 1)
                    v.wait_ge(sems["v"], base["v"] + vn + 2)
                    v.tensor_scalar_mul(mneg[:, :], mrow[:, :], -SCALE).then_inc(sems["v"], 1)
                    v.wait_ge(sems["e"], base["e"] + n + 1)
                    v.reciprocal(zinv[:, :], zrow[:, :]).then_inc(sems["v"], 1)
                    v.wait_ge(sems["v"], base["v"] + vn + 4)
                    v.tensor_scalar_mul(E4[:, t4 * S: t4 * S + jlen],
                                        E4[:, t4 * S: t4 * S + jlen],
                                        zinv[:, :]).then_inc(sems["v"], 1)
                    vn += NVT
                for jb in range(jmax):
                    v.wait_ge(sems["tt"], base["tt"] + ttn + 1)
                    if ttn >= 2:
                        v.wait_ge(sems["avj"], base["avj"] + ttn - 1)
                    v.tensor_copy(out=ET[ttn % 2][:, :],
                                  in_=psET[ttn % 2][:, 0:CW]).then_inc(sems["te"], 1)
                    ttn += 1
            v.wait_ge(sems["avj"], base["avj"] + 2 * jmax)
            v.tensor_copy(out=avout[:, :], in_=psA[:, 0:CW]).then_inc(sems["ao"], 1)

        @blk.scalar
        def _(s):
            for hh in range(2):
                for t4 in range(CT):
                    n = hh * CT + t4
                    it = CT * ic + t4
                    jlen = (it + 1) * 128
                    s.wait_ge(sems["v"], base["v"] + n * NVT + 3)
                    s.activation(out=E4[:, t4 * S: t4 * S + jlen],
                                 in_=scores[:, 0:jlen], func=AF.Exp,
                                 scale=SCALE, bias=mrow[:, :] if False else mneg[:, :],
                                 accum_out=zrow[:, :]).then_inc(sems["e"], 1)

        @blk.gpsimd
        def _(g):
            g.wait_ge(sems["ao"], base["ao"] + 1)
            g.dma_start(out=d["attnT"][hp * 128:(hp + 1) * 128, ic * CW:(ic + 1) * CW],
                        in_=avout[:, :]).then_inc(sems["st"], 16)
            g.wait_ge(sems["st"], base["st"] + 16)

    nc.all_engine_barrier()
    nb = dict(base)
    nmm = sum(nch(t4) for t4 in range(CT)) * 2
    nb["mm"] += nmm
    nb["sc"] += nmm
    nb["v"] += 2 * CT * NVT
    nb["e"] += 2 * CT
    nb["tt"] += 2 * jmax
    nb["te"] += 2 * jmax
    nb["avj"] += 2 * jmax
    nb["ao"] += 1
    nb["st"] += 16
    return nb


def emit_bias_cols(nc, consts, les, bias_d, bcol, brow, name):
    L = bias_d.shape[0]
    ktn = L // 128
    bes = ExitStack()
    les = bes
    psB = les.enter_context(nc.psum_tensor(f"{name}_psB", [128, 512], F32))
    sld = sem(nc, f"{name}_bld")
    stt = sem(nc, f"{name}_btt")
    with bes, nc.Block(f"{name}_bias") as blk:
        @blk.gpsimd
        def _(g):
            g.dma_start(out=brow[:, :], in_=bias_d[None, :]).then_inc(sld, 16)

        @blk.tensor
        def _(t):
            t.wait_ge(sld, 16)
            for rep in range(2):
                for kt in range(ktn):
                    tr = t.transpose(psB[:, kt:kt + 1], brow[0:1, kt * 128:(kt + 1) * 128],
                                     consts["ones_row"][0:1, 0:1])
                tr.then_inc(stt, 1)

        @blk.vector
        def _(v):
            v.wait_ge(stt, 2)
            vdelay(v, consts, reps=4)
            v.tensor_copy(out=bcol[:, :], in_=psB[:, 0:ktn])
    nc.all_engine_barrier()


def emit_weight_transpose(nc, consts, src_d, wT_sb, name, nr=8, ncol=8):
    es = ExitStack()
    with es:
        w_sb = es.enter_context(nc.sbuf_tensor(f"{name}_w", [128, nr * ncol * 128], BF16))
        psW = [es.enter_context(nc.psum_tensor(f"{name}_ps{i}", [128, 1024], BF16)) for i in range(2)]
        sld = sem(nc, f"{name}_ld")
        stt = sem(nc, f"{name}_tt")
        sev = sem(nc, f"{name}_ev")
        with nc.Block(name) as blk:
            @blk.gpsimd
            def _(g):
                cast_load(g, w_sb, src_d, sld)

            @blk.tensor
            def _(t):
                t.wait_ge(sld, 16)
                for ct in range(ncol):
                    if ct >= 2:
                        t.wait_ge(sev, ct - 1)
                    for rt in range(nr):
                        tr = t.transpose(psW[ct % 2][:, rt * 128:(rt + 1) * 128],
                                         w_sb[:, rt * ncol * 128 + ct * 128: rt * ncol * 128 + (ct + 1) * 128],
                                         consts["ident_bf"][:, :])
                    tr.then_inc(stt, 1)

            @blk.vector
            def _(v):
                for ct in range(ncol):
                    v.wait_ge(stt, ct + 1)
                    v.tensor_copy(out=wT_sb[:, ct * nr * 128:(ct + 1) * nr * 128],
                                  in_=psW[ct % 2][:, 0:nr * 128]).then_inc(sev, 1)
        nc.all_engine_barrier()


def build_phase_c(nc, es, d, consts):
    SCALE = 1.0 / 32.0

    emit_ln_pass(nc, consts, lambda it: d["x2"][it * 128:(it + 1) * 128, :],
                 NT, None, name="c_ln", out_lnT_dram=d["lnT"])

    kT = es.enter_context(nc.sbuf_tensor("c_kT", [128, KT * S], BF16))
    vrow = es.enter_context(nc.sbuf_tensor("c_vrow", [128, NT * D], BF16))

    for wname, bias_name, kind in (("wq", "bq", "q"), ("wk", "bk", "k"), ("wv", "bv", "v")):
        with ExitStack() as les:
            wT = les.enter_context(nc.sbuf_tensor(f"c_{wname}T", [128, KT * D], BF16))
            emit_weight_transpose(nc, consts, d[wname], wT, f"c_t{wname}")
            lnc = [les.enter_context(nc.sbuf_tensor(f"c_ln{wname}{i}", [128, KT * 512], BF16))
                   for i in range(2)]
            stage = [les.enter_context(nc.sbuf_tensor(f"c_st{wname}{i}", [128, 512], BF16))
                     for i in range(2)]
            bcol = les.enter_context(nc.sbuf_tensor(f"c_b{wname}", [128, KT], F32))
            brow = les.enter_context(nc.sbuf_tensor(f"c_br{wname}", [1, D], F32))
            psP = [les.enter_context(nc.psum_tensor(f"c_psP{wname}{i}", [128, 512], F32)) for i in range(2)]
            sld = sem(nc, f"c_{wname}_ld")
            smm = sem(nc, f"c_{wname}_mm")
            sev = sem(nc, f"c_{wname}_ev")
            sst = sem(nc, f"c_{wname}_st")
            emit_bias_cols(nc, consts, les, d[bias_name], bcol, brow, f"c_{wname}")
            with nc.Block(f"c_p{wname}") as blk:
                @blk.gpsimd
                def _(g):
                    g.dma_start(out=brow[:, :], in_=d[bias_name][None, :]).then_inc(sld, 16)
                    for ic in range(8):
                        if ic >= 2:
                            g.wait_ge(smm, (ic - 1) * 8)
                        g.dma_start(
                            out=lnc[ic % 2][:, :].rearrange("p (kt i) -> p kt i", kt=KT),
                            in_=d["lnT"].rearrange("(kt p) s -> p kt s", p=128)[:, :, ic * 512:(ic + 1) * 512]
                        ).then_inc(sld, 16)
                        if kind == "q" and ic >= 1:
                            for dq in range(8):
                                n = (ic - 1) * 8 + dq
                                g.wait_ge(sev, n + 1)
                                g.dma_start(out=d["qT"][dq * 128:(dq + 1) * 128, (ic - 1) * 512:ic * 512],
                                            in_=stage[n % 2][:, :]).then_inc(sst, 16)
                    if kind == "q":
                        for dq in range(8):
                            n = 7 * 8 + dq
                            g.wait_ge(sev, n + 1)
                            g.dma_start(out=d["qT"][dq * 128:(dq + 1) * 128, 7 * 512:8 * 512],
                                        in_=stage[n % 2][:, :]).then_inc(sst, 16)
                        g.wait_ge(sst, 16 * 64)

                if kind in ("q", "k"):
                    @blk.tensor
                    def _(t):
                        n = 0
                        for ic in range(8):
                            t.wait_ge(sld, 16 + 16 * (ic + 1))
                            for dq in range(8):
                                if n >= 2:
                                    t.wait_ge(sev, n - 1)
                                for kt in range(KT):
                                    mm = t.matmul(psP[n % 2][:, :],
                                                  wT[:, kt * D + dq * 128: kt * D + (dq + 1) * 128],
                                                  lnc[ic % 2][:, kt * 512:(kt + 1) * 512],
                                                  start=(kt == 0), stop=(kt == KT - 1))
                                mm.then_inc(smm, 1)
                                n += 1

                    @blk.vector
                    def _(v):
                        n = 0
                        for ic in range(8):
                            for dq in range(8):
                                v.wait_ge(smm, n + 1)
                                if kind == "q" and n >= 2:
                                    v.wait_ge(sst, 16 * (n - 1))
                                if kind == "q":
                                    dst = stage[n % 2][:, :]
                                else:
                                    dst = kT[:, dq * S + ic * 512:(dq) * S + ic * 512 + 512]
                                v.tensor_scalar(dst, psP[n % 2][:, :], bcol[:, dq:dq + 1],
                                                None, ALU.add).then_inc(sev, 1)
                                n += 1
                else:
                    @blk.tensor
                    def _(t):
                        n = 0
                        for ic in range(8):
                            t.wait_ge(sld, 16 + 16 * (ic + 1))
                            for it4 in range(4):
                                ioff = it4 * 128
                                for half in range(2):
                                    if n >= 2:
                                        t.wait_ge(sev, n - 1)
                                    for kt in range(KT):
                                        t.matmul(psP[n % 2][:, :],
                                                 lnc[ic % 2][:, kt * 512 + ioff: kt * 512 + ioff + 128],
                                                 wT[:, kt * D + half * 512: kt * D + (half + 1) * 512],
                                                 start=(kt == 0), stop=False)
                                    mm = t.matmul(psP[n % 2][:, :], consts["ones_row"][0:1, 0:128],
                                                  brow[0:1, half * 512:(half + 1) * 512],
                                                  start=False, stop=True)
                                    mm.then_inc(smm, 1)
                                    n += 1

                    @blk.vector
                    def _(v):
                        n = 0
                        for ic in range(8):
                            for it4 in range(4):
                                it = ic * 4 + it4
                                for half in range(2):
                                    v.wait_ge(smm, n + 1)
                                    v.tensor_copy(
                                        out=vrow[:, it * D + half * 512: it * D + (half + 1) * 512],
                                        in_=psP[n % 2][:, :]).then_inc(sev, 1)
                                    n += 1
            nc.all_engine_barrier()

    with ExitStack() as les:
        qTt = [les.enter_context(nc.sbuf_tensor(f"c_qTt{i}", [128, KT * 128], BF16)) for i in range(2)]
        scores = les.enter_context(nc.sbuf_tensor("c_scores", [128, S], F32))
        Erow = les.enter_context(nc.sbuf_tensor("c_E", [128, S], BF16))
        ET = [les.enter_context(nc.sbuf_tensor(f"c_ET{i}", [128, 128], BF16)) for i in range(2)]
        orow = les.enter_context(nc.sbuf_tensor("c_orow", [128, D], BF16))
        ostg = [les.enter_context(nc.sbuf_tensor(f"c_ostg{i}", [128, KT * 128], BF16)) for i in range(2)]
        mrow = les.enter_context(nc.sbuf_tensor("c_mrow", [128, 1], F32))
        mneg = les.enter_context(nc.sbuf_tensor("c_mneg", [128, 1], F32))
        zrow = les.enter_context(nc.sbuf_tensor("c_zrow", [128, 1], F32))
        zinv = les.enter_context(nc.sbuf_tensor("c_zinv", [128, 1], F32))
        psS = [les.enter_context(nc.psum_tensor(f"c_psS{i}", [128, 512], F32)) for i in range(2)]
        psET = [les.enter_context(nc.psum_tensor(f"c_psET{i}", [128, 1024], BF16)) for i in range(2)]
        psR = [les.enter_context(nc.psum_tensor(f"c_psR{i}", [128, 512], F32)) for i in range(2)]
        sems = {nm: sem(nc, f"c_{nm}")
                for nm in ["ld", "mm", "sc", "v", "e", "tt", "te", "avj", "re", "ot", "oe", "st"]}
        base = {nm: 0 for nm in sems}
        for it in range(NT):
            base = _emit_c_attn_tile(nc, consts, d, it, qTt, kT, vrow, scores, Erow,
                                     ET, orow, ostg, mrow, mneg, zrow, zinv,
                                     psS, psET, psR, sems, base)

    with ExitStack() as les:
        woT = les.enter_context(nc.sbuf_tensor("c_woT", [128, KT * D], BF16))
        emit_weight_transpose(nc, consts, d["wo"], woT, "c_two")
        _emit_residual_proj(nc, consts, les, d["outT"], woT, d["bo"], d["x2"], d["x3"], "c_x3")
    nc.all_engine_barrier()


def _emit_c_attn_tile(nc, consts, d, it, qTt, kT, vrow, scores, Erow, ET, orow,
                      ostg, mrow, mneg, zrow, zinv, psS, psET, psR, sems, base):
    SCALE = 1.0 / 32.0
    jlen = (it + 1) * 128
    jmax = it + 1
    nchunks = cdiv(jlen, 512)
    NVT = 5

    with nc.Block(f"c_at{it}") as blk:
        @blk.gpsimd
        def _(g):
            g.dma_start(out=qTt[it % 2][:, :].rearrange("p (kt i) -> p kt i", kt=KT),
                        in_=d["qT"].rearrange("(kt p) s -> p kt s", p=128)[:, :, it * 128:(it + 1) * 128]
                        ).then_inc(sems["ld"], 16)

        @blk.tensor
        def _(t):
            t.wait_ge(sems["ld"], base["ld"] + 16)
            mmn = 0
            for c in range(nchunks):
                if mmn >= 2:
                    t.wait_ge(sems["sc"], base["sc"] + mmn - 1)
                for kt in range(KT):
                    mm = t.matmul(psS[mmn % 2][:, :],
                                  qTt[it % 2][:, kt * 128:(kt + 1) * 128],
                                  kT[:, kt * S + c * 512: kt * S + (c + 1) * 512],
                                  start=(kt == 0), stop=(kt == KT - 1))
                mm.then_inc(sems["mm"], 1)
                mmn += 1
            for jb in range(jmax):
                bt = jb % 2
                if jb >= 2:
                    t.wait_ge(sems["te"], base["te"] + jb - 1)
                t.wait_ge(sems["v"], base["v"] + 5)
                t.transpose(psET[bt][:, 0:128],
                            Erow[:, jb * 128:(jb + 1) * 128],
                            consts["ident_bf"][:, :]).then_inc(sems["tt"], 1)
                t.wait_ge(sems["te"], base["te"] + jb + 1)
                for half in range(2):
                    t.matmul(psR[half][:, :], ET[bt][:, :],
                             vrow[:, jb * D + half * 512: jb * D + (half + 1) * 512],
                             start=(jb == 0), stop=(jb == jmax - 1)).then_inc(sems["avj"], 1)
            t.wait_ge(sems["re"], base["re"] + 2)
            for kt in range(KT):
                tr = t.transpose(psET[0][:, kt * 128:(kt + 1) * 128],
                                 orow[:, kt * 128:(kt + 1) * 128],
                                 consts["ident_bf"][:, :])
            tr.then_inc(sems["ot"], 1)

        @blk.vector
        def _(v):
            mmn = nchunks
            v.wait_ge(sems["sc"], base["sc"] + mmn)
            v.tensor_tensor(scores[:, jlen - 128: jlen], scores[:, jlen - 128: jlen],
                            consts["causal_bias"][:, :], ALU.add).then_inc(sems["v"], 1)
            v.wait_ge(sems["v"], base["v"] + 1)
            v.reduce_max(mrow[:, :], scores[:, 0:jlen], axis=AX.X).then_inc(sems["v"], 1)
            v.wait_ge(sems["v"], base["v"] + 2)
            v.tensor_scalar_mul(mneg[:, :], mrow[:, :], -SCALE).then_inc(sems["v"], 1)
            v.wait_ge(sems["e"], base["e"] + 1)
            v.reciprocal(zinv[:, :], zrow[:, :]).then_inc(sems["v"], 1)
            v.wait_ge(sems["v"], base["v"] + 4)
            v.tensor_scalar_mul(Erow[:, 0:jlen], Erow[:, 0:jlen], zinv[:, :]).then_inc(sems["v"], 1)
            for jb in range(jmax):
                v.wait_ge(sems["tt"], base["tt"] + jb + 1)
                if jb >= 2:
                    v.wait_ge(sems["avj"], base["avj"] + 2 * (jb - 1))
                v.tensor_copy(out=ET[jb % 2][:, :], in_=psET[jb % 2][:, 0:128]).then_inc(sems["te"], 1)
            v.wait_ge(sems["avj"], base["avj"] + 2 * jmax)
            for half in range(2):
                v.tensor_copy(out=orow[:, half * 512:(half + 1) * 512],
                              in_=psR[half][:, :]).then_inc(sems["re"], 1)
            v.wait_ge(sems["ot"], base["ot"] + 1)
            if it >= 2:
                v.wait_ge(sems["st"], base["st"] - 16)
            v.tensor_copy(out=ostg[it % 2][:, :], in_=psET[0][:, :]).then_inc(sems["oe"], 1)

        @blk.scalar
        def _(s):
            for c in range(nchunks):
                s.wait_ge(sems["mm"], base["mm"] + c + 1)
                if c == 0:
                    s.activation(out=consts["vd_scr"][:, :], in_=consts["ident_f32"][:, :],
                                 func=AF.Abs)
                    s.activation(out=consts["vd_scr"][:, :], in_=consts["ident_f32"][:, :],
                                 func=AF.Abs)
                cl = min(512, jlen - c * 512)
                s.activation(out=scores[:, c * 512: c * 512 + cl],
                             in_=psS[c % 2][:, 0:cl], func=AF.Abs).then_inc(sems["sc"], 1)
            s.wait_ge(sems["v"], base["v"] + 3)
            s.activation(out=Erow[:, 0:jlen], in_=scores[:, 0:jlen], func=AF.Exp,
                         scale=SCALE, bias=mneg[:, :], accum_out=zrow[:, :]).then_inc(sems["e"], 1)

        @blk.gpsimd
        def _(g):
            g.wait_ge(sems["oe"], base["oe"] + 1)
            g.dma_start(
                out=d["outT"].rearrange("(kt p) s -> p kt s", p=128)[:, :, it * 128:(it + 1) * 128],
                in_=ostg[it % 2][:, :].rearrange("p (kt i) -> p kt i", kt=KT)).then_inc(sems["st"], 16)

    nc.all_engine_barrier()
    nb = dict(base)
    nb["ld"] += 16
    nb["mm"] += nchunks
    nb["sc"] += nchunks
    nb["v"] += NVT
    nb["e"] += 1
    nb["tt"] += jmax
    nb["te"] += jmax
    nb["avj"] += 2 * jmax
    nb["re"] += 2
    nb["ot"] += 1
    nb["oe"] += 1
    nb["st"] += 16
    return nb


def _emit_residual_proj(nc, consts, les, inT_dram, wT_sb, bias_d, res_dram, out_dram,
                        name, scale_in=None, extra_add_dram=None):
    bo_bc = les.enter_context(nc.sbuf_tensor(f"{name}_bobc", [128, D], F32))
    bo_row = les.enter_context(nc.sbuf_tensor(f"{name}_borow", [1, D], F32))
    inT = [les.enter_context(nc.sbuf_tensor(f"{name}_inT{i}", [128, KT * 128], BF16)) for i in range(2)]
    rest = [les.enter_context(nc.sbuf_tensor(f"{name}_res{i}", [128, D], F32)) for i in range(2)]
    extt = None
    if extra_add_dram is not None:
        extt = [les.enter_context(nc.sbuf_tensor(f"{name}_ext{i}", [128, D], F32)) for i in range(2)]
    outt = [les.enter_context(nc.sbuf_tensor(f"{name}_out{i}", [128, D], F32)) for i in range(2)]
    psX = [les.enter_context(nc.psum_tensor(f"{name}_psX{i}", [128, 512], F32)) for i in range(2)]
    sld = sem(nc, f"{name}_ld")
    smm = sem(nc, f"{name}_mm")
    sev = sem(nc, f"{name}_ev")
    svv = sem(nc, f"{name}_vv")
    sst = sem(nc, f"{name}_st")
    NLD = 32 if extt is None else 48

    with nc.Block(f"{name}_pre") as blk:
        @blk.gpsimd
        def _(g):
            g.dma_start(out=bo_row[:, :], in_=bias_d[None, :]).then_inc(sld, 16)
            g.wait_ge(sld, 16)

        @blk.tensor
        def _(t):
            t.wait_ge(sld, 16)
            for half in range(2):
                t.matmul(psX[half][:, :], consts["ones_row"][0:1, 0:128],
                         bo_row[0:1, half * 512:(half + 1) * 512],
                         start=True, stop=True).then_inc(smm, 1)

        @blk.vector
        def _(v):
            for half in range(2):
                v.wait_ge(smm, half + 1)
                v.tensor_copy(out=bo_bc[:, half * 512:(half + 1) * 512], in_=psX[half][:, :])
    nc.all_engine_barrier()

    def _ldst(g, it):
        b = it % 2
        if it >= 2:
            g.wait_ge(smm, 2 + (it - 2) * 2 + 2)
            g.wait_ge(sev, (it - 2) * 2 + 2)
        g.dma_start(out=inT[b][:, :].rearrange("p (kt i) -> p kt i", kt=KT),
                    in_=inT_dram.rearrange("(kt p) s -> p kt s", p=128)[:, :, it * 128:(it + 1) * 128]
                    ).then_inc(sld, 16)
        g.dma_start(out=rest[b][:, :],
                    in_=res_dram[it * 128:(it + 1) * 128, :]).then_inc(sld, 16)
        if extt is not None:
            g.dma_start(out=extt[b][:, :],
                        in_=extra_add_dram[it * 128:(it + 1) * 128, :]).then_inc(sld, 16)

    with nc.Block(f"{name}_main") as blk:
        @blk.gpsimd
        def _(g):
            for it in range(NT):
                _ldst(g, it)
                if it >= 1:
                    g.wait_ge(sev, (it - 1) * 2 + 2)
                    g.dma_start(out=out_dram[(it - 1) * 128:it * 128, :],
                                in_=outt[(it - 1) % 2][:, :]).then_inc(sst, 16)
            g.wait_ge(sev, NT * 2)
            g.dma_start(out=out_dram[(NT - 1) * 128:NT * 128, :],
                        in_=outt[(NT - 1) % 2][:, :]).then_inc(sst, 16)
            g.wait_ge(sst, 16 * NT)

        @blk.tensor
        def _(t):
            for it in range(NT):
                b = it % 2
                t.wait_ge(sld, 16 + NLD * (it + 1))
                for half in range(2):
                    if it >= 1:
                        t.wait_ge(sev, (it - 1) * 2 + half + 1)
                    for kt in range(KT):
                        mm = t.matmul(psX[half][:, :],
                                      inT[b][:, kt * 128:(kt + 1) * 128],
                                      wT_sb[:, kt * D + half * 512: kt * D + (half + 1) * 512],
                                      start=(kt == 0), stop=(kt == KT - 1))
                    mm.then_inc(smm, 1)

        @blk.vector
        def _(v):
            nv = 0
            for it in range(NT):
                b = it % 2
                for half in range(2):
                    v.wait_ge(smm, 2 + it * 2 + half + 1)
                    if it >= 2:
                        v.wait_ge(sst, 16 * (it - 1))
                    sl = slice(half * 512, (half + 1) * 512)
                    v.tensor_tensor(outt[b][:, sl], psX[half][:, :], rest[b][:, sl],
                                    ALU.add).then_inc(svv, 1)
                    nv += 1
                    v.wait_ge(svv, nv)
                    if extt is not None:
                        v.tensor_tensor(outt[b][:, sl], outt[b][:, sl], extt[b][:, sl],
                                        ALU.add).then_inc(svv, 1)
                        nv += 1
                        v.wait_ge(svv, nv)
                    v.tensor_tensor(outt[b][:, sl], outt[b][:, sl], bo_bc[:, sl],
                                    ALU.add).then_inc(sev, 1)

    nc.all_engine_barrier()


def build_phase_d(nc, es, d, consts, dstage=99):
    H, KPH = 16, 32
    SC = 1.0 / np.sqrt(32.0)

    emit_ln_pass(nc, consts, lambda it: d["x3"][it * 128:(it + 1) * 128, :],
                 NT, None, name="d_ln", out_lnT_dram=d["hsT"],
                 extra_row_fn=lambda it: d["hs"][it * 128:(it + 1) * 128, :])

    logits_sb = es.enter_context(nc.sbuf_tensor("d_lg", [128, NT * 8], F32))
    cmb_sb = es.enter_context(nc.sbuf_tensor("d_cmb", [128, NT * 8], F32))

    if dstage < 2:
        return
    att_es = ExitStack()
    kT = att_es.enter_context(nc.sbuf_tensor("d_kT", [128, KT * S], BF16))
    vrow = att_es.enter_context(nc.sbuf_tensor("d_vrow", [128, NT * 512], BF16))
    for kind in ("q", "k", "v"):
        with ExitStack() as les:
            w_d = d[f"{kind}_w"]
            ncol = w_d.shape[1] // 128
            w_sb = les.enter_context(nc.sbuf_tensor(f"d_w{kind}", [128, KT * ncol * 128], BF16))
            lnc = [les.enter_context(nc.sbuf_tensor(f"d_ln{kind}{i}", [128, KT * 512], BF16))
                   for i in range(2)]
            stage = [les.enter_context(nc.sbuf_tensor(f"d_st{kind}{i}", [128, 512], BF16))
                     for i in range(2)]
            bcol = les.enter_context(nc.sbuf_tensor(f"d_bc{kind}", [128, max(ncol, 1)], F32))
            brow = les.enter_context(nc.sbuf_tensor(f"d_br{kind}", [1, ncol * 128], F32))
            psP = [les.enter_context(nc.psum_tensor(f"d_psP{kind}{i}", [128, 512], F32)) for i in range(2)]
            sld = sem(nc, f"d_{kind}_ld")
            smm = sem(nc, f"d_{kind}_mm")
            sev = sem(nc, f"d_{kind}_ev")
            sst = sem(nc, f"d_{kind}_st")
            emit_bias_cols(nc, consts, les, d[f"{kind}_b"], bcol, brow, f"d_{kind}")
            if kind == "q":
                ngrp = ncol
            elif kind == "k":
                ngrp = 8
            else:
                ngrp = 4
            with nc.Block(f"d_p{kind}") as blk:
                @blk.gpsimd
                def _(g):
                    cast_load(g, w_sb, w_d, sld)
                    for ic in range(8):
                        if ic >= 2:
                            g.wait_ge(smm, (ic - 1) * ngrp)
                        g.dma_start(
                            out=lnc[ic % 2][:, :].rearrange("p (kt i) -> p kt i", kt=KT),
                            in_=d["hsT"].rearrange("(kt p) s -> p kt s", p=128)[:, :, ic * 512:(ic + 1) * 512]
                        ).then_inc(sld, 16)
                        if kind == "q" and ic >= 1:
                            for dq in range(ncol):
                                n = (ic - 1) * ncol + dq
                                g.wait_ge(sev, n + 1)
                                g.dma_start(out=d["qT"][dq * 128:(dq + 1) * 128, (ic - 1) * 512:ic * 512],
                                            in_=stage[n % 2][:, :]).then_inc(sst, 16)
                    if kind == "q":
                        for dq in range(ncol):
                            n = 7 * ncol + dq
                            g.wait_ge(sev, n + 1)
                            g.dma_start(out=d["qT"][dq * 128:(dq + 1) * 128, 7 * 512:8 * 512],
                                        in_=stage[n % 2][:, :]).then_inc(sst, 16)
                        g.wait_ge(sst, 16 * 8 * ncol)

                if kind == "q":
                    @blk.tensor
                    def _(t):
                        n = 0
                        for ic in range(8):
                            t.wait_ge(sld, 16 + 16 * (ic + 1))
                            for dq in range(ncol):
                                if n >= 2:
                                    t.wait_ge(sev, n - 1)
                                for kt in range(KT):
                                    mm = t.matmul(psP[n % 2][:, :],
                                                  w_sb[:, kt * ncol * 128 + dq * 128: kt * ncol * 128 + (dq + 1) * 128],
                                                  lnc[ic % 2][:, kt * 512:(kt + 1) * 512],
                                                  start=(kt == 0), stop=(kt == KT - 1))
                                mm.then_inc(smm, 1)
                                n += 1

                    @blk.vector
                    def _(v):
                        n = 0
                        for ic in range(8):
                            for dq in range(ncol):
                                v.wait_ge(smm, n + 1)
                                if n >= 2:
                                    v.wait_ge(sst, 16 * (n - 1))
                                dst = stage[n % 2][:, :]
                                v.tensor_scalar(dst, psP[n % 2][:, :], bcol[:, dq:dq + 1],
                                                None, ALU.add).then_inc(sev, 1)
                                n += 1

                elif kind == "k":
                    @blk.tensor
                    def _(t):
                        n = 0
                        for ic in range(8):
                            t.wait_ge(sld, 16 + 16 * (ic + 1))
                            for ht in range(8):
                                if n >= 2:
                                    t.wait_ge(sev, n - 1)
                                for hh in range(2):
                                    h = 2 * ht + hh
                                    bp = 64 * hh
                                    for kt in range(KT):
                                        t.matmul(psP[n % 2][bp:bp + 32, :],
                                                 w_sb[:, kt * ncol * 128 + h * 32: kt * ncol * 128 + h * 32 + 32],
                                                 lnc[ic % 2][:, kt * 512:(kt + 1) * 512],
                                                 start=(kt == 0), stop=False,
                                                 tile_position=(0, bp))
                                    mm = t.matmul(psP[n % 2][bp:bp + 32, :],
                                                  brow[0:1, h * 32: h * 32 + 32],
                                                  consts["ones_row"][0:1, 0:512],
                                                  start=False, stop=True,
                                                  tile_position=(0, bp))
                                mm.then_inc(smm, 1)
                                n += 1

                    @blk.vector
                    def _(v):
                        n = 0
                        for ic in range(8):
                            for ht in range(8):
                                v.wait_ge(smm, n + 1)
                                for hh in range(2):
                                    bp = 64 * hh
                                    v.tensor_copy(
                                        out=kT[bp:bp + 32, ht * S + ic * 512: ht * S + ic * 512 + 512],
                                        in_=psP[n % 2][bp:bp + 32, :]).then_inc(sev, 1)
                                n += 1
                else:
                    @blk.tensor
                    def _(t):
                        n = 0
                        for ic in range(8):
                            t.wait_ge(sld, 16 + 16 * (ic + 1))
                            for it4 in range(4):
                                ioff = it4 * 128
                                for half in range(1):
                                    if n >= 2:
                                        t.wait_ge(sev, n - 1)
                                    for kt in range(KT):
                                        t.matmul(psP[n % 2][:, :],
                                                 lnc[ic % 2][:, kt * 512 + ioff: kt * 512 + ioff + 128],
                                                 w_sb[:, kt * 512: kt * 512 + 512],
                                                 start=(kt == 0), stop=False)
                                    mm = t.matmul(psP[n % 2][:, :], consts["ones_row"][0:1, 0:128],
                                                  brow[0:1, 0:512], start=False, stop=True)
                                    mm.then_inc(smm, 1)
                                    n += 1

                    @blk.vector
                    def _(v):
                        n = 0
                        for ic in range(8):
                            for it4 in range(4):
                                it = ic * 4 + it4
                                v.wait_ge(smm, n + 1)
                                v.tensor_copy(out=vrow[:, it * 512:(it + 1) * 512],
                                              in_=psP[n % 2][:, :]).then_inc(sev, 1)
                                n += 1
            nc.all_engine_barrier()

    if dstage < 3:
        att_es.close()
        return
    with ExitStack() as les:
        qTt = [les.enter_context(nc.sbuf_tensor(f"d_qTt{i}", [128, KT * 128], BF16)) for i in range(2)]
        scores = les.enter_context(nc.sbuf_tensor("d_scores", [128, S], F32))
        E4h = les.enter_context(nc.sbuf_tensor("d_E4h", [128, 4 * S], BF16))
        ET4 = [les.enter_context(nc.sbuf_tensor(f"d_ET{i}", [128, 512], BF16)) for i in range(2)]
        aost = [les.enter_context(nc.sbuf_tensor(f"d_aost{i}", [128, 128], BF16)) for i in range(2)]
        mrow = les.enter_context(nc.sbuf_tensor("d_mrow", [128, 1], F32))
        mneg = les.enter_context(nc.sbuf_tensor("d_mneg", [128, 1], F32))
        zrow = les.enter_context(nc.sbuf_tensor("d_zrow", [128, 1], F32))
        zinv = les.enter_context(nc.sbuf_tensor("d_zinv", [128, 1], F32))
        psS = [les.enter_context(nc.psum_tensor(f"d_psS{i}", [128, 512], F32)) for i in range(2)]
        psET = [les.enter_context(nc.psum_tensor(f"d_psET{i}", [128, 1024], BF16)) for i in range(2)]
        psA = les.enter_context(nc.psum_tensor("d_psA", [128, 512], F32))
        sems = {nm: sem(nc, f"d_{nm}")
                for nm in ["ld", "mm", "sc", "v", "e", "tt", "te", "avj", "ao", "st"]}
        base = {nm: 0 for nm in sems}
        for it in range(NT):
            with nc.Block(f"d_at{it}_ld") as blk:
                @blk.gpsimd
                def _(g):
                    g.dma_start(out=qTt[it % 2][:, :].rearrange("p (kt i) -> p kt i", kt=KT),
                                in_=d["qT"].rearrange("(kt p) s -> p kt s", p=128)[:, :, it * 128:(it + 1) * 128]
                                ).then_inc(sems["ld"], 16)
            for g in range(4):
                base = _emit_d_attn_group(nc, consts, d, it, g, qTt[it % 2], kT, vrow,
                                          scores, E4h, ET4, aost, mrow, mneg, zrow,
                                          zinv, psS, psET, psA, sems, base, SC)
            base["ld"] += 16

    att_es.close()

    if dstage < 4:
        return
    with ExitStack() as les:
        wo_sb = les.enter_context(nc.sbuf_tensor("d_wo", [128, 4 * D], BF16))
        rw_sb = les.enter_context(nc.sbuf_tensor("d_rw", [128, KT * 8], BF16))
        borow = les.enter_context(nc.sbuf_tensor("d_borow", [1, D], F32))
        rbrow = les.enter_context(nc.sbuf_tensor("d_rbrow", [1, 8], F32))
        aot = [les.enter_context(nc.sbuf_tensor(f"d_aot{i}", [128, 4 * 128], BF16)) for i in range(2)]
        mrow_bf = les.enter_context(nc.sbuf_tensor("d_mlarow", [128, D], BF16))
        mstg = [les.enter_context(nc.sbuf_tensor(f"d_mstg{i}", [128, KT * 128], BF16)) for i in range(2)]
        psX = [les.enter_context(nc.psum_tensor(f"d_psX{i}", [128, 512], F32)) for i in range(2)]
        psT = les.enter_context(nc.psum_tensor("d_psTm", [128, 1024], BF16))
        psL = les.enter_context(nc.psum_tensor("d_psL", [128, 512], F32))
        sld = sem(nc, "d_m_ld")
        smm = sem(nc, "d_m_mm")
        sev = sem(nc, "d_m_ev")
        stt = sem(nc, "d_m_tt")
        sme = sem(nc, "d_m_me")
        slg = sem(nc, "d_m_lg")
        sle = sem(nc, "d_m_le")
        sst = sem(nc, "d_m_st")
        with nc.Block("d_mla_w") as blk:
            @blk.gpsimd
            def _(g):
                cast_load(g, wo_sb, d["o_w"][0:512, :], sld)
                cast_load(g, rw_sb, d["router_w"], sld)
                g.dma_start(out=borow[:, :], in_=d["o_b"][None, :]).then_inc(sld, 16)
                g.dma_start(out=rbrow[:, :], in_=d["router_b"][None, :]).then_inc(sld, 16)
                g.wait_ge(sld, 64)
        nc.all_engine_barrier()

        with nc.Block("d_mla") as blk:
            @blk.gpsimd
            def _(g):
                for it in range(NT):
                    if it >= 2:
                        g.wait_ge(smm, (it - 2) * 2 + 2)
                    g.dma_start(out=aot[it % 2][:, :].rearrange("p (kt i) -> p kt i", kt=4),
                                in_=d["aoT"].rearrange("(kt p) s -> p kt s", p=128)[:, :, it * 128:(it + 1) * 128]
                                ).then_inc(sld, 16)
                    if it >= 1:
                        pit = it - 1
                        g.wait_ge(sme, pit + 1)
                        g.dma_start(
                            out=d["mlaT"].rearrange("(kt p) s -> p kt s", p=128)[:, :, pit * 128:(pit + 1) * 128],
                            in_=mstg[pit % 2][:, :].rearrange("p (kt i) -> p kt i", kt=KT)
                        ).then_inc(sst, 16)
                pit = NT - 1
                g.wait_ge(sme, pit + 1)
                g.dma_start(
                    out=d["mlaT"].rearrange("(kt p) s -> p kt s", p=128)[:, :, pit * 128:(pit + 1) * 128],
                    in_=mstg[pit % 2][:, :].rearrange("p (kt i) -> p kt i", kt=KT)
                ).then_inc(sst, 16)
                g.wait_ge(sst, 16 * NT)

            @blk.tensor
            def _(t):
                for it in range(NT):
                    b = it % 2
                    t.wait_ge(sld, 64 + 16 * (it + 1))
                    for half in range(2):
                        if it >= 1:
                            t.wait_ge(sev, (it - 1) * 2 + half + 1)
                        for kt in range(4):
                            t.matmul(psX[half][:, :],
                                     aot[b][:, kt * 128:(kt + 1) * 128],
                                     wo_sb[:, kt * D + half * 512: kt * D + (half + 1) * 512],
                                     start=(kt == 0), stop=False)
                        mm = t.matmul(psX[half][:, :], consts["ones_row"][0:1, 0:128],
                                      borow[0:1, half * 512:(half + 1) * 512],
                                      start=False, stop=True)
                        mm.then_inc(smm, 1)
                    t.wait_ge(sev, it * 2 + 2)
                    for kt in range(KT):
                        tr = t.transpose(psT[:, kt * 128:(kt + 1) * 128],
                                         mrow_bf[:, kt * 128:(kt + 1) * 128],
                                         consts["ident_bf"][:, :])
                    tr.then_inc(stt, 1)
                    t.wait_ge(sme, it + 1)
                    if it >= 1:
                        t.wait_ge(sle, it)
                    for kt in range(KT):
                        t.matmul(psL[:, 0:8],
                                 mstg[it % 2][:, kt * 128:(kt + 1) * 128],
                                 rw_sb[:, kt * 8:(kt + 1) * 8],
                                 start=(kt == 0), stop=False)
                    mm = t.matmul(psL[:, 0:8], consts["ones_row"][0:1, 0:128],
                                  rbrow[0:1, 0:8], start=False, stop=True)
                    mm.then_inc(slg, 1)

            @blk.vector
            def _(v):
                for it in range(NT):
                    b = it % 2
                    for half in range(2):
                        v.wait_ge(smm, it * 2 + half + 1)
                        v.tensor_copy(out=mrow_bf[:, half * 512:(half + 1) * 512],
                                      in_=psX[half][:, :]).then_inc(sev, 1)
                    v.wait_ge(stt, it + 1)
                    if it >= 2:
                        v.wait_ge(sst, 16 * (it - 1))
                    v.tensor_copy(out=mstg[b][:, :], in_=psT[:, :]).then_inc(sme, 1)
                    v.wait_ge(slg, it + 1)
                    vdelay(v, consts)
                    v.tensor_copy(out=logits_sb[:, it * 8:(it + 1) * 8],
                                  in_=psL[:, 0:8]).then_inc(sle, 1)

        nc.all_engine_barrier()

    if dstage < 5:
        return
    with ExitStack() as les:
        t8 = [les.enter_context(nc.sbuf_tensor(f"d_r{i}", [128, 8], F32)) for i in range(5)]
        (mask1, l2, mask2, e8m, e8dump) = t8
        sc1 = [les.enter_context(nc.sbuf_tensor(f"d_rs{i}", [128, 1], F32)) for i in range(10)]
        (m1, m2, m1n, e2, za, d0, den, dinv, c2v, zrow_r) = sc1
        sv = sem(nc, "d_r_v")
        ss = sem(nc, "d_r_s")
        NVR = 13
        with nc.Block("d_route") as blk:
            @blk.vector
            def _(v):
                for it in range(NT):
                    lg = logits_sb[:, it * 8:(it + 1) * 8]
                    n0 = it * NVR
                    if it >= 1:
                        v.wait_ge(ss, 2 * it)
                    v.reduce_max(m1[:, :], lg, axis=AX.X).then_inc(sv, 1)
                    v.wait_ge(sv, n0 + 1)
                    v.tensor_scalar(mask1[:, :], lg, m1[:, :], None, ALU.is_equal).then_inc(sv, 1)
                    v.wait_ge(sv, n0 + 2)
                    v.scalar_tensor_tensor(out=l2[:, :], in0=mask1[:, :], scalar=-1e30,
                                           in1=lg, op0=ALU.mult, op1=ALU.add).then_inc(sv, 1)
                    v.wait_ge(sv, n0 + 3)
                    v.reduce_max(m2[:, :], l2[:, :], axis=AX.X).then_inc(sv, 1)
                    v.wait_ge(sv, n0 + 4)
                    v.tensor_scalar(mask2[:, :], l2[:, :], m2[:, :], None, ALU.is_equal).then_inc(sv, 1)
                    v.tensor_scalar_mul(m1n[:, :], m1[:, :], -1.0).then_inc(sv, 1)
                    v.wait_ge(ss, it * 2 + 1)
                    v.tensor_scalar_mul(za[:, :], zrow_r[:, :], 1e-8).then_inc(sv, 1)
                    v.wait_ge(ss, it * 2 + 2)
                    v.tensor_scalar(d0[:, :], e2[:, :], 1.0, None, ALU.add).then_inc(sv, 1)
                    v.wait_ge(sv, n0 + 8)
                    v.tensor_tensor(den[:, :], d0[:, :], za[:, :], ALU.add).then_inc(sv, 1)
                    v.wait_ge(sv, n0 + 9)
                    v.reciprocal(dinv[:, :], den[:, :]).then_inc(sv, 1)
                    v.wait_ge(sv, n0 + 10)
                    v.tensor_tensor(c2v[:, :], e2[:, :], dinv[:, :], ALU.mult).then_inc(sv, 1)
                    v.tensor_scalar(e8m[:, :], mask1[:, :], dinv[:, :], None, ALU.mult).then_inc(sv, 1)
                    v.wait_ge(sv, n0 + 12)
                    v.scalar_tensor_tensor(out=cmb_sb[:, it * 8:(it + 1) * 8],
                                           in0=mask2[:, :], scalar=c2v[:, :],
                                           in1=e8m[:, :], op0=ALU.mult, op1=ALU.add).then_inc(sv, 1)

            @blk.scalar
            def _(s):
                for it in range(NT):
                    lg = logits_sb[:, it * 8:(it + 1) * 8]
                    s.wait_ge(sv, it * NVR + 6)
                    s.activation(out=e8dump[:, :], in_=lg, func=AF.Exp,
                                 bias=m1n[:, :], accum_out=zrow_r[:, :]).then_inc(ss, 1)
                    s.wait_ge(ss, it * 2 + 1)
                    s.activation(out=e2[:, :], in_=m2[:, :], func=AF.Exp,
                                 bias=m1n[:, :]).then_inc(ss, 1)
        nc.all_engine_barrier()

    if dstage < 6:
        return
    with ExitStack() as les:
        zt = les.enter_context(nc.sbuf_tensor("d_zero", [128, D], F32))
        sz = sem(nc, "d_zz")
        with nc.Block("d_zero_moe") as blk:
            @blk.gpsimd
            def _(g):
                g.memset(zt[:, :], 0.0).then_inc(sz, 1)
                g.wait_ge(sz, 1)
                for it in range(NT):
                    g.dma_start(out=d["moe"][it * 128:(it + 1) * 128, :],
                                in_=zt[:, :]).then_inc(sz, 16)
                g.wait_ge(sz, 1 + 16 * NT)
        nc.all_engine_barrier()

    FT2 = 16
    with ExitStack() as les:
        w1_sb = les.enter_context(nc.sbuf_tensor("d_w1", [128, KT * 2048], BF16))
        w2_sb = les.enter_context(nc.sbuf_tensor("d_w2", [128, FT2 * D], BF16))
        at_sb = les.enter_context(nc.sbuf_tensor("d_at", [128, FT2 * 512], BF16))
        mlc = [les.enter_context(nc.sbuf_tensor(f"d_mlc{i}", [128, KT * 512], BF16)) for i in range(2)]
        b1col = les.enter_context(nc.sbuf_tensor("d_b1c", [128, FT2], F32))
        b1row = les.enter_context(nc.sbuf_tensor("d_b1r", [1, 2048], F32))
        b2row = les.enter_context(nc.sbuf_tensor("d_b2r", [1, D], F32))
        tsb = [les.enter_context(nc.sbuf_tensor(f"d_tsb{i}", [128, 512], F32)) for i in range(2)]
        psA1 = [les.enter_context(nc.psum_tensor(f"d_psA1{i}", [128, 512], F32)) for i in range(2)]
        psH = [les.enter_context(nc.psum_tensor(f"d_psH{i}", [128, 512], F32)) for i in range(4)]
        sems = {nm: sem(nc, f"d_e_{nm}")
                for nm in ["wl", "ld", "m1", "ev1", "m2", "ev2", "st"]}
        base = {nm: 0 for nm in sems}
        for e in range(8):
            base = _emit_d_expert(nc, consts, d, e, w1_sb, w2_sb, at_sb, mlc,
                                  b1col, b1row, b2row, tsb, psA1, psH, cmb_sb,
                                  les, sems, base)

    if dstage < 8:
        return
    with ExitStack() as les:
        x3t = [les.enter_context(nc.sbuf_tensor(f"d_x3t{i}", [128, D], F32)) for i in range(2)]
        hst = [les.enter_context(nc.sbuf_tensor(f"d_hst{i}", [128, D], F32)) for i in range(2)]
        moet = [les.enter_context(nc.sbuf_tensor(f"d_moet{i}", [128, D], F32)) for i in range(2)]
        x4t = [les.enter_context(nc.sbuf_tensor(f"d_x4t{i}", [128, D], F32)) for i in range(2)]
        sld = sem(nc, "d_f_ld")
        svv = sem(nc, "d_f_vv")
        sev = sem(nc, "d_f_ev")
        sst = sem(nc, "d_f_st")
        with nc.Block("d_final") as blk:
            @blk.gpsimd
            def _(g):
                for it in range(NT):
                    b = it % 2
                    if it >= 2:
                        g.wait_ge(sev, it - 1)
                    g.dma_start(out=x3t[b][:, :], in_=d["x3"][it * 128:(it + 1) * 128, :]).then_inc(sld, 16)
                    g.dma_start(out=hst[b][:, :], in_=d["hs"][it * 128:(it + 1) * 128, :]).then_inc(sld, 16)
                    g.dma_start(out=moet[b][:, :], in_=d["moe"][it * 128:(it + 1) * 128, :]).then_inc(sld, 16)
                    if it >= 1:
                        g.wait_ge(sev, it)
                        g.dma_start(out=d["x4"][(it - 1) * 128:it * 128, :],
                                    in_=x4t[(it - 1) % 2][:, :]).then_inc(sst, 16)
                g.wait_ge(sev, NT)
                g.dma_start(out=d["x4"][(NT - 1) * 128:NT * 128, :],
                            in_=x4t[(NT - 1) % 2][:, :]).then_inc(sst, 16)
                g.wait_ge(sst, 16 * NT)

            @blk.vector
            def _(v):
                for it in range(NT):
                    b = it % 2
                    v.wait_ge(sld, 48 * (it + 1))
                    if it >= 2:
                        v.wait_ge(sst, 16 * (it - 1))
                    v.scalar_tensor_tensor(out=x4t[b][:, :], in0=moet[b][:, :], scalar=0.3,
                                           in1=hst[b][:, :], op0=ALU.mult, op1=ALU.add).then_inc(svv, 1)
                    v.wait_ge(svv, it + 1)
                    v.tensor_tensor(x4t[b][:, :], x4t[b][:, :], x3t[b][:, :], ALU.add).then_inc(sev, 1)

        nc.all_engine_barrier()


def _emit_d_attn_group(nc, consts, d, it, g, qTt, kT, vrow, scores, E4h, ET4,
                       aost, mrow, mneg, zrow, zinv, psS, psET, psA, sems, base, SC):
    NVT = 4
    nch = 8

    with nc.Block(f"d_at{it}_{g}") as blk:
        @blk.tensor
        def _(t):
            mmn = 0
            for hh in range(4):
                h = 4 * g + hh
                qsl = qTt[64 * (h % 2): 64 * (h % 2) + 32, (h // 2) * 128: (h // 2) * 128 + 128]
                t.wait_ge(sems["ld"], base["ld"] + 16)
                for c in range(nch):
                    if mmn >= 2:
                        t.wait_ge(sems["sc"], base["sc"] + mmn - 1)
                    t.matmul(psS[mmn % 2][:, :], qsl,
                             kT[64 * (h % 2): 64 * (h % 2) + 32,
                                (h // 2) * S + c * 512: (h // 2) * S + (c + 1) * 512],
                             start=True, stop=True).then_inc(sems["mm"], 1)
                    mmn += 1
            ttn = 0
            for jb in range(NT):
                bt = ttn % 2
                if ttn >= 2:
                    t.wait_ge(sems["te"], base["te"] + ttn - 1)
                for hh in range(4):
                    t.wait_ge(sems["v"], base["v"] + hh * NVT + 4)
                    tr = t.transpose(psET[bt][:, hh * 128:(hh + 1) * 128],
                                     E4h[:, hh * S + jb * 128: hh * S + (jb + 1) * 128],
                                     consts["ident_bf"][:, :])
                tr.then_inc(sems["tt"], 1)
                ttn += 1
                t.wait_ge(sems["te"], base["te"] + ttn)
                for hh in range(4):
                    h = 4 * g + hh
                    t.matmul(psA[32 * hh: 32 * (hh + 1), 0:128],
                             vrow[:, jb * 512 + 32 * h: jb * 512 + 32 * h + 32],
                             ET4[(ttn - 1) % 2][:, hh * 128:(hh + 1) * 128],
                             start=(jb == 0), stop=(jb == NT - 1),
                             tile_position=(0, 32 * hh)).then_inc(sems["avj"], 1)

        @blk.vector
        def _(v):
            mmn = 0
            for hh in range(4):
                n = hh
                v.wait_ge(sems["e"], base["e"] + n)
                for c in range(nch):
                    v.wait_ge(sems["mm"], base["mm"] + mmn + 1)
                    v.tensor_copy(out=scores[:, c * 512:(c + 1) * 512],
                                  in_=psS[mmn % 2][:, :]).then_inc(sems["sc"], 1)
                    mmn += 1
                v.wait_ge(sems["sc"], base["sc"] + mmn)
                v.reduce_max(mrow[:, :], scores[:, :], axis=AX.X).then_inc(sems["v"], 1)
                v.wait_ge(sems["v"], base["v"] + n * NVT + 1)
                v.tensor_scalar_mul(mneg[:, :], mrow[:, :], -SC).then_inc(sems["v"], 1)
                v.wait_ge(sems["e"], base["e"] + n + 1)
                v.reciprocal(zinv[:, :], zrow[:, :]).then_inc(sems["v"], 1)
                v.wait_ge(sems["v"], base["v"] + n * NVT + 3)
                v.tensor_scalar_mul(E4h[:, hh * S:(hh + 1) * S], E4h[:, hh * S:(hh + 1) * S],
                                    zinv[:, :]).then_inc(sems["v"], 1)
            ttn = 0
            for jb in range(NT):
                v.wait_ge(sems["tt"], base["tt"] + ttn + 1)
                if ttn >= 2:
                    v.wait_ge(sems["avj"], base["avj"] + 4 * (ttn - 1))
                v.tensor_copy(out=ET4[ttn % 2][:, :], in_=psET[ttn % 2][:, 0:512]).then_inc(sems["te"], 1)
                ttn += 1
            v.wait_ge(sems["avj"], base["avj"] + 4 * NT)
            if it * 4 + g >= 2:
                v.wait_ge(sems["st"], base["st"] - 16)
            v.tensor_copy(out=aost[(it * 4 + g) % 2][:, :], in_=psA[:, 0:128]).then_inc(sems["ao"], 1)

        @blk.scalar
        def _(s):
            for hh in range(4):
                s.wait_ge(sems["v"], base["v"] + hh * NVT + 2)
                s.activation(out=E4h[:, hh * S:(hh + 1) * S], in_=scores[:, :], func=AF.Exp,
                             scale=SC, bias=mneg[:, :], accum_out=zrow[:, :]).then_inc(sems["e"], 1)

        @blk.gpsimd
        def _(gg):
            gg.wait_ge(sems["ao"], base["ao"] + 1)
            gg.dma_start(out=d["aoT"][128 * g:128 * (g + 1), it * 128:(it + 1) * 128],
                         in_=aost[(it * 4 + g) % 2][:, :]).then_inc(sems["st"], 16)

    nc.all_engine_barrier()
    nb = dict(base)
    nb["mm"] += 32
    nb["sc"] += 32
    nb["v"] += 4 * NVT
    nb["e"] += 4
    nb["tt"] += NT
    nb["te"] += NT
    nb["avj"] += 4 * NT
    nb["ao"] += 1
    nb["st"] += 16
    return nb


def _emit_d_expert(nc, consts, d, e, w1_sb, w2_sb, at_sb, mlc, b1col, b1row,
                   b2row, tsb, psA1, psH, cmb_sb, les, sems, base):
    FT2 = 16
    emit_bias_cols(nc, consts, les, d["b1"][e, :], b1col, b1row, f"d_e{e}b1")
    with nc.Block(f"d_e{e}_w") as blk:
        @blk.gpsimd
        def _(g):
            cast_load(g, w1_sb, d["w1"][e], sems["wl"])
            cast_load(g, w2_sb, d["w2"][e], sems["wl"])
            g.dma_start(out=b2row[:, :], in_=d["b2"][e][None, :]).then_inc(sems["wl"], 16)
            g.wait_ge(sems["wl"], base["wl"] + 48)
    nc.all_engine_barrier()
    base = dict(base)
    base["wl"] += 48

    def _d_exp_stores(g, ic):
        for ip in range(4):
            tile = ic * 4 + ip
            n2 = ic * 4 + ip
            g.wait_ge(sems["ev2"], base["ev2"] + 2 * (n2 + 1))
            g.dma_start(out=d["moe"][tile * 128:(tile + 1) * 128, 0:512],
                        in_=tsb[0][:, :], accum_op=ALU.add).then_inc(sems["st"], 16)
            g.dma_start(out=d["moe"][tile * 128:(tile + 1) * 128, 512:1024],
                        in_=tsb[1][:, :], accum_op=ALU.add).then_inc(sems["st"], 16)

    with nc.Block(f"d_e{e}") as blk:
        @blk.gpsimd
        def _(g):
            for ic in range(8):
                if ic >= 2:
                    g.wait_ge(sems["m1"], base["m1"] + (ic - 1) * FT2)
                g.dma_start(
                    out=mlc[ic % 2][:, :].rearrange("p (kt i) -> p kt i", kt=KT),
                    in_=d["mlaT"].rearrange("(kt p) s -> p kt s", p=128)[:, :, ic * 512:(ic + 1) * 512]
                ).then_inc(sems["ld"], 16)
                if ic >= 1:
                    _d_exp_stores(g, ic - 1)
            _d_exp_stores(g, 7)
            g.wait_ge(sems["st"], base["st"] + 32 * 32)

        @blk.tensor
        def _(t):
            n1 = 0
            n2 = 0
            for ic in range(8):
                t.wait_ge(sems["ld"], base["ld"] + 16 * (ic + 1))
                for f in range(FT2):
                    if n1 >= 2:
                        t.wait_ge(sems["ev1"], base["ev1"] + n1 - 1)
                    for kt in range(KT):
                        mm = t.matmul(psA1[n1 % 2][:, :],
                                      w1_sb[:, kt * 2048 + f * 128: kt * 2048 + (f + 1) * 128],
                                      mlc[ic % 2][:, kt * 512:(kt + 1) * 512],
                                      start=(kt == 0), stop=(kt == KT - 1))
                    mm.then_inc(sems["m1"], 1)
                    n1 += 1
                t.wait_ge(sems["ev1"], base["ev1"] + (ic + 1) * FT2)
                for ip in range(4):
                    pb0, pb1 = psH[2 * (ip % 2)], psH[2 * (ip % 2) + 1]
                    if n2 >= 2:
                        t.wait_ge(sems["ev2"], base["ev2"] + 2 * (n2 - 1))
                    for f in range(FT2):
                        lhs = at_sb[:, f * 512 + ip * 128: f * 512 + (ip + 1) * 128]
                        t.matmul(pb0[:, :], lhs, w2_sb[:, f * D: f * D + 512],
                                 start=(f == 0), stop=False)
                        t.matmul(pb1[:, :], lhs, w2_sb[:, f * D + 512:(f + 1) * D],
                                 start=(f == 0), stop=False)
                    t.matmul(pb0[:, :], consts["ones_row"][0:1, 0:128],
                             b2row[0:1, 0:512], start=False, stop=True)
                    mm = t.matmul(pb1[:, :], consts["ones_row"][0:1, 0:128],
                                  b2row[0:1, 512:1024], start=False, stop=True)
                    mm.then_inc(sems["m2"], 1)
                    n2 += 1

        @blk.scalar
        def _(s):
            n1 = 0
            for ic in range(8):
                for f in range(FT2):
                    s.wait_ge(sems["m1"], base["m1"] + n1 + 1)
                    s.activation(out=at_sb[:, f * 512:(f + 1) * 512],
                                 in_=psA1[n1 % 2][:, :], func=AF.Gelu,
                                 bias=b1col[:, f:f + 1]).then_inc(sems["ev1"], 1)
                    n1 += 1

        @blk.vector
        def _(v):
            n2 = 0
            for ic in range(8):
                for ip in range(4):
                    tile = ic * 4 + ip
                    pb0, pb1 = psH[2 * (ip % 2)], psH[2 * (ip % 2) + 1]
                    v.wait_ge(sems["m2"], base["m2"] + n2 + 1)
                    if n2 >= 1:
                        v.wait_ge(sems["st"], base["st"] + 32 * n2)
                    v.tensor_scalar(tsb[0][:, :], pb0[:, :],
                                    cmb_sb[:, tile * 8 + e: tile * 8 + e + 1],
                                    None, ALU.mult).then_inc(sems["ev2"], 1)
                    v.tensor_scalar(tsb[1][:, :], pb1[:, :],
                                    cmb_sb[:, tile * 8 + e: tile * 8 + e + 1],
                                    None, ALU.mult).then_inc(sems["ev2"], 1)
                    n2 += 1

    nc.all_engine_barrier()
    nb = dict(base)
    nb["ld"] += 16 * 8
    nb["m1"] += 8 * FT2
    nb["ev1"] += 8 * FT2
    nb["m2"] += 32
    nb["ev2"] += 64
    nb["st"] += 32 * 32
    return nb


def emit_transpose_pass(nc, consts, src_rowtile_fn, n_tiles, out_T_dram, name):
    es = ExitStack()
    with es:
        xt = [es.enter_context(nc.sbuf_tensor(f"{name}_xt{i}", [128, D], F32)) for i in range(2)]
        xb = [es.enter_context(nc.sbuf_tensor(f"{name}_xb{i}", [128, D], BF16)) for i in range(2)]
        stage = [es.enter_context(nc.sbuf_tensor(f"{name}_st{i}", [128, KT * 128], BF16)) for i in range(2)]
        psT = [es.enter_context(nc.psum_tensor(f"{name}_ps{i}", [128, 1024], BF16)) for i in range(2)]
        sld = sem(nc, f"{name}_ld")
        scv = sem(nc, f"{name}_cv")
        stt = sem(nc, f"{name}_tt")
        sev = sem(nc, f"{name}_ev")
        sst = sem(nc, f"{name}_st")
        with nc.Block(name) as blk:
            @blk.gpsimd
            def _(g):
                for it in range(n_tiles):
                    if it >= 2:
                        g.wait_ge(scv, it - 1)
                    g.dma_start(out=xt[it % 2][:, :], in_=src_rowtile_fn(it)).then_inc(sld, 16)
                    if it >= 1:
                        pit = it - 1
                        g.wait_ge(sev, pit + 1)
                        g.dma_start(
                            out=out_T_dram.rearrange("(kt p) s -> p kt s", p=128)[:, :, pit * 128:(pit + 1) * 128],
                            in_=stage[pit % 2][:, :].rearrange("p (kt i) -> p kt i", kt=KT)
                        ).then_inc(sst, 16)
                pit = n_tiles - 1
                g.wait_ge(sev, pit + 1)
                g.dma_start(
                    out=out_T_dram.rearrange("(kt p) s -> p kt s", p=128)[:, :, pit * 128:(pit + 1) * 128],
                    in_=stage[pit % 2][:, :].rearrange("p (kt i) -> p kt i", kt=KT)
                ).then_inc(sst, 16)
                g.wait_ge(sst, 16 * n_tiles)

            @blk.vector
            def _(v):
                for it in range(n_tiles):
                    b = it % 2
                    v.wait_ge(sld, 16 * (it + 1))
                    if it >= 2:
                        v.wait_ge(stt, it - 1)
                    v.tensor_copy(out=xb[b][:, :], in_=xt[b][:, :]).then_inc(scv, 1)
                    v.wait_ge(stt, it + 1)
                    if it >= 2:
                        v.wait_ge(sst, 16 * (it - 1))
                    v.tensor_copy(out=stage[b][:, :], in_=psT[b][:, :]).then_inc(sev, 1)

            @blk.tensor
            def _(t):
                for it in range(n_tiles):
                    b = it % 2
                    t.wait_ge(scv, it + 1)
                    if it >= 2:
                        t.wait_ge(sev, it - 1)
                    for kt in range(KT):
                        tr = t.transpose(psT[b][:, kt * 128:(kt + 1) * 128],
                                         xb[b][:, kt * 128:(kt + 1) * 128],
                                         consts["ident_bf"][:, :])
                    tr.then_inc(stt, 1)

        nc.all_engine_barrier()


def build_phase_b(nc, es, d, consts):
    NL, NE, FF = 4, 16, 4096
    FT = FF // 128

    emit_ln_pass(nc, consts, lambda it: d["x1"][it * 128:(it + 1) * 128, :],
                 NT, None, name="b_ln", out_lnT_dram=d["xtT0"])

    sel_sb = es.enter_context(nc.sbuf_tensor("b_sel", [128, NT * NE], F32))
    dbg = d.get("dbg_sel") if isinstance(d, dict) else None

    for l in range(NL):
        xtT = d["xtT0"] if l % 2 == 0 else d["xtT1"]
        xtT_next = d["xtT1"] if l % 2 == 0 else d["xtT0"]
        _emit_b_router(nc, consts, d, l, xtT, sel_sb)
        if dbg is not None:
            _ues = ExitStack()
            with _ues:
                _sst = sem(nc, f"b_dbgs{l}")
                with nc.Block(f"b_dbg{l}") as _bk:
                    @_bk.gpsimd
                    def _(g):
                        g.dma_start(out=dbg[l], in_=sel_sb[:, :]).then_inc(_sst, 16)
                        g.wait_ge(_sst, 16)
            nc.all_engine_barrier()
        _emit_b_zero_acc(nc, d)
        with ExitStack() as les:
            w1_sb = les.enter_context(nc.sbuf_tensor(f"b_w1_{l}", [128, KT * FF], BF16))
            w2_sb = les.enter_context(nc.sbuf_tensor(f"b_w2_{l}", [128, FT * D], BF16))
            at_sb = les.enter_context(nc.sbuf_tensor(f"b_at_{l}", [128, FT * 512], BF16))
            xtc = [les.enter_context(nc.sbuf_tensor(f"b_xtc{l}_{i}", [128, KT * 512], BF16)) for i in range(2)]
            tsb = [les.enter_context(nc.sbuf_tensor(f"b_tsb{l}_{i}", [128, 512], F32)) for i in range(2)]
            psA1 = [les.enter_context(nc.psum_tensor(f"b_psA1{l}_{i}", [128, 512], F32)) for i in range(2)]
            psH = [les.enter_context(nc.psum_tensor(f"b_psH{l}_{i}", [128, 512], F32)) for i in range(4)]
            sems = {nm: sem(nc, f"b_l{l}_{nm}")
                    for nm in ["wl", "ld", "m1", "ev1", "m2", "ev2", "st"]}
            base = {nm: 0 for nm in sems}
            for e in range(NE):
                base = _emit_b_expert(nc, consts, d, l, e, xtT, w1_sb, w2_sb, at_sb,
                                      xtc, tsb, psA1, psH, sel_sb, sems, base)
        if l < NL - 1:
            emit_transpose_pass(nc, consts,
                                lambda it: d["acc"][it * 128:(it + 1) * 128, :],
                                NT, xtT_next, f"b_tp{l}")

    with ExitStack() as les:
        x1t = [les.enter_context(nc.sbuf_tensor(f"b_x1t{i}", [128, D], F32)) for i in range(2)]
        act = [les.enter_context(nc.sbuf_tensor(f"b_act{i}", [128, D], F32)) for i in range(2)]
        x2t = [les.enter_context(nc.sbuf_tensor(f"b_x2t{i}", [128, D], F32)) for i in range(2)]
        sld = sem(nc, "b_f_ld")
        sev = sem(nc, "b_f_ev")
        sst = sem(nc, "b_f_st")
        with nc.Block("b_final") as blk:
            @blk.gpsimd
            def _(g):
                for it in range(NT):
                    b = it % 2
                    if it >= 2:
                        g.wait_ge(sev, it - 1)
                    g.dma_start(out=x1t[b][:, :], in_=d["x1"][it * 128:(it + 1) * 128, :]).then_inc(sld, 16)
                    g.dma_start(out=act[b][:, :], in_=d["acc"][it * 128:(it + 1) * 128, :]).then_inc(sld, 16)
                    if it >= 1:
                        g.wait_ge(sev, it)
                        g.dma_start(out=d["x2"][(it - 1) * 128:it * 128, :],
                                    in_=x2t[(it - 1) % 2][:, :]).then_inc(sst, 16)
                g.wait_ge(sev, NT)
                g.dma_start(out=d["x2"][(NT - 1) * 128:NT * 128, :],
                            in_=x2t[(NT - 1) % 2][:, :]).then_inc(sst, 16)
                g.wait_ge(sst, 16 * NT)

            @blk.vector
            def _(v):
                for it in range(NT):
                    b = it % 2
                    v.wait_ge(sld, 32 * (it + 1))
                    if it >= 2:
                        v.wait_ge(sst, 16 * (it - 1))
                    v.tensor_tensor(x2t[b][:, :], x1t[b][:, :], act[b][:, :], ALU.add).then_inc(sev, 1)

        nc.all_engine_barrier()


DBG_LG = None
DBG_Z = None
DBG_XTC = None


def _emit_b_router(nc, consts, d, l, xtT, sel_sb):
    NE = 16
    es = ExitStack()
    with es:
        rw_sb = es.enter_context(nc.sbuf_tensor(f"b_rw{l}", [128, KT * NE], BF16))
        xtc = [es.enter_context(nc.sbuf_tensor(f"b_rx{l}_{i}", [128, KT * 512], BF16)) for i in range(2)]
        lg = es.enter_context(nc.sbuf_tensor(f"b_lgt{l}", [128, NE], F32))
        m1 = es.enter_context(nc.sbuf_tensor(f"b_m1_{l}", [128, 1], F32))
        m1n = es.enter_context(nc.sbuf_tensor(f"b_m1n{l}", [128, 1], F32))
        mask = es.enter_context(nc.sbuf_tensor(f"b_mask{l}", [128, NE], F32))
        zr = es.enter_context(nc.sbuf_tensor(f"b_zr{l}", [128, 1], F32))
        zi = es.enter_context(nc.sbuf_tensor(f"b_zi{l}", [128, 1], F32))
        edump = es.enter_context(nc.sbuf_tensor(f"b_ed{l}", [128, NE], F32))
        psL = es.enter_context(nc.psum_tensor(f"b_psL{l}", [128, 512], F32))
        sld = sem(nc, f"b_r_ld{l}")
        smm = sem(nc, f"b_r_mm{l}")
        sv = sem(nc, f"b_r_v{l}")
        ss = sem(nc, f"b_r_s{l}")
        NVR = 5
        tiles = [(0, 0, True)] + [(ic, it4, False) for ic in range(8) for it4 in range(4)]
        selscr = es.enter_context(nc.sbuf_tensor(f"b_selscr{l}", [128, NE], F32))

        with nc.Block(f"b_rt{l}") as blk:
            @blk.gpsimd
            def _(g):
                cast_load(g, rw_sb, d["routers"][l], sld)
                for ic in range(8):
                    if ic >= 2:
                        g.wait_ge(smm, 4 * ic - 3)
                    g.dma_start(
                        out=xtc[ic % 2][:, :].rearrange("p (kt i) -> p kt i", kt=KT),
                        in_=xtT.rearrange("(kt p) s -> p kt s", p=128)[:, :, ic * 512:(ic + 1) * 512]
                    ).then_inc(sld, 16)

            @blk.tensor
            def _(t):
                for n, (ic, it4, warm) in enumerate(tiles):
                    t.wait_ge(sld, 16 + 16 * (ic + 1))
                    if n >= 1:
                        t.wait_ge(sv, n * NVR - NVR + 1)
                    for kt in range(KT):
                        mm = t.matmul(psL[:, 0:NE],
                                      xtc[ic % 2][:, kt * 512 + it4 * 128: kt * 512 + it4 * 128 + 128],
                                      rw_sb[:, kt * NE:(kt + 1) * NE],
                                      start=(kt == 0), stop=(kt == KT - 1))
                    mm.then_inc(smm, 1)

            @blk.vector
            def _(v):
                for n, (ic, it4, warm) in enumerate(tiles):
                    it = ic * 4 + it4
                    n0 = n * NVR
                    if n >= 1:
                        v.wait_ge(ss, n)
                    v.wait_ge(smm, n + 1)
                    vdelay(v, consts)
                    v.tensor_copy(out=lg[:, :], in_=psL[:, 0:NE]).then_inc(sv, 1)
                    v.wait_ge(sv, n0 + 1)
                    v.reduce_max(m1[:, :], lg[:, :], axis=AX.X).then_inc(sv, 1)
                    v.wait_ge(sv, n0 + 2)
                    v.tensor_scalar(mask[:, :], lg[:, :], m1[:, :], None, ALU.is_equal)
                    v.tensor_scalar_mul(m1n[:, :], m1[:, :], -1.0).then_inc(sv, 1)
                    v.wait_ge(ss, n + 1)
                    v.reciprocal(zi[:, :], zr[:, :]).then_inc(sv, 1)
                    v.wait_ge(sv, n0 + 4)
                    dst = selscr[:, :] if warm else sel_sb[:, it * NE:(it + 1) * NE]
                    v.tensor_scalar(dst, mask[:, :], zi[:, :], None, ALU.mult).then_inc(sv, 1)

            @blk.scalar
            def _(s):
                for n in range(33):
                    s.wait_ge(sv, n * NVR + 3)
                    s.activation(out=edump[:, :], in_=lg[:, :], func=AF.Exp,
                                 bias=m1n[:, :], accum_out=zr[:, :]).then_inc(ss, 1)
        nc.all_engine_barrier()


_b_zero_ctr = [0]


def _emit_b_zero_acc(nc, d):
    _b_zero_ctr[0] += 1
    uid = _b_zero_ctr[0]
    es = ExitStack()
    with es:
        zt = es.enter_context(nc.sbuf_tensor(f"b_zero{uid}", [128, D], F32))
        sz = sem(nc, f"b_zz{uid}")
        with nc.Block(f"b_zacc{uid}") as blk:
            @blk.gpsimd
            def _(g):
                g.memset(zt[:, :], 0.0).then_inc(sz, 1)
                g.wait_ge(sz, 1)
                for it in range(NT):
                    g.dma_start(out=d["acc"][it * 128:(it + 1) * 128, :],
                                in_=zt[:, :]).then_inc(sz, 16)
                g.wait_ge(sz, 1 + 16 * NT)
        nc.all_engine_barrier()


def _emit_b_expert(nc, consts, d, l, e, xtT, w1_sb, w2_sb, at_sb, xtc, tsb,
                   psA1, psH, sel_sb, sems, base):
    FT = 32
    NE = 16
    with nc.Block(f"b_l{l}e{e}_w") as blk:
        @blk.gpsimd
        def _(g):
            cast_load(g, w1_sb, d["w1"][e], sems["wl"])
            cast_load(g, w2_sb, d["w2"][e], sems["wl"])
            g.wait_ge(sems["wl"], base["wl"] + 32)
    nc.all_engine_barrier()
    base = dict(base)
    base["wl"] += 32

    def _b_exp_stores(g, ic, n2base):
        for ip in range(4):
            tile = ic * 4 + ip
            n2 = ic * 4 + ip
            g.wait_ge(sems["ev2"], base["ev2"] + 2 * (n2 + 1))
            g.dma_start(out=d["acc"][tile * 128:(tile + 1) * 128, 0:512],
                        in_=tsb[0][:, :], accum_op=ALU.add).then_inc(sems["st"], 16)
            g.dma_start(out=d["acc"][tile * 128:(tile + 1) * 128, 512:1024],
                        in_=tsb[1][:, :], accum_op=ALU.add).then_inc(sems["st"], 16)

    with nc.Block(f"b_l{l}e{e}") as blk:
        @blk.gpsimd
        def _(g):
            for ic in range(8):
                if ic >= 2:
                    g.wait_ge(sems["m1"], base["m1"] + (ic - 1) * FT)
                g.dma_start(
                    out=xtc[ic % 2][:, :].rearrange("p (kt i) -> p kt i", kt=KT),
                    in_=xtT.rearrange("(kt p) s -> p kt s", p=128)[:, :, ic * 512:(ic + 1) * 512]
                ).then_inc(sems["ld"], 16)
                if ic >= 1:
                    _b_exp_stores(g, ic - 1, 0)
            _b_exp_stores(g, 7, 0)
            g.wait_ge(sems["st"], base["st"] + 32 * 32)

        @blk.tensor
        def _(t):
            n1 = 0
            n2 = 0
            for ic in range(8):
                t.wait_ge(sems["ld"], base["ld"] + 16 * (ic + 1))
                for f in range(FT):
                    if n1 >= 2:
                        t.wait_ge(sems["ev1"], base["ev1"] + n1 - 1)
                    for kt in range(KT):
                        mm = t.matmul(psA1[n1 % 2][:, :],
                                      w1_sb[:, kt * 4096 + f * 128: kt * 4096 + (f + 1) * 128],
                                      xtc[ic % 2][:, kt * 512:(kt + 1) * 512],
                                      start=(kt == 0), stop=(kt == KT - 1))
                    mm.then_inc(sems["m1"], 1)
                    n1 += 1
                t.wait_ge(sems["ev1"], base["ev1"] + (ic + 1) * FT)
                for ip in range(4):
                    pb0, pb1 = psH[2 * (ip % 2)], psH[2 * (ip % 2) + 1]
                    if n2 >= 2:
                        t.wait_ge(sems["ev2"], base["ev2"] + 2 * (n2 - 1))
                    for f in range(FT):
                        lhs = at_sb[:, f * 512 + ip * 128: f * 512 + (ip + 1) * 128]
                        t.matmul(pb0[:, :], lhs, w2_sb[:, f * D: f * D + 512],
                                 start=(f == 0), stop=(f == FT - 1))
                        mm = t.matmul(pb1[:, :], lhs, w2_sb[:, f * D + 512:(f + 1) * D],
                                      start=(f == 0), stop=(f == FT - 1))
                    mm.then_inc(sems["m2"], 1)
                    n2 += 1

        @blk.scalar
        def _(s):
            n1 = 0
            for ic in range(8):
                for f in range(FT):
                    s.wait_ge(sems["m1"], base["m1"] + n1 + 1)
                    s.activation(out=at_sb[:, f * 512:(f + 1) * 512],
                                 in_=psA1[n1 % 2][:, :], func=AF.Silu).then_inc(sems["ev1"], 1)
                    n1 += 1

        @blk.vector
        def _(v):
            n2 = 0
            for ic in range(8):
                for ip in range(4):
                    tile = ic * 4 + ip
                    pb0, pb1 = psH[2 * (ip % 2)], psH[2 * (ip % 2) + 1]
                    v.wait_ge(sems["m2"], base["m2"] + n2 + 1)
                    if n2 >= 1:
                        v.wait_ge(sems["st"], base["st"] + 32 * n2)
                    v.tensor_scalar(tsb[0][:, :], pb0[:, :],
                                    sel_sb[:, tile * NE + e: tile * NE + e + 1],
                                    None, ALU.mult).then_inc(sems["ev2"], 1)
                    v.tensor_scalar(tsb[1][:, :], pb1[:, :],
                                    sel_sb[:, tile * NE + e: tile * NE + e + 1],
                                    None, ALU.mult).then_inc(sems["ev2"], 1)
                    n2 += 1

    nc.all_engine_barrier()
    nb = dict(base)
    nb["ld"] += 16 * 8
    nb["m1"] += 8 * FT
    nb["ev1"] += 8 * FT
    nb["m2"] += 32
    nb["ev2"] += 64
    nb["st"] += 32 * 32
    return nb


from concourse.bass_utils import run_bass_kernel_spmd

NCORES = 8


def _build_a():
    global _SEM_ES
    nc = bass.Bass("TRN2", target_bir_lowering=False)
    _SEM_ES = ExitStack()
    aps = dict(
        x0=nc.dram_tensor("x0", [S, D], F32, kind="ExternalInput").ap(),
        wq=nc.dram_tensor("wq", [D, D], F32, kind="ExternalInput").ap(),
        wkv=nc.dram_tensor("wkv", [D, 2 * D], F32, kind="ExternalInput").ap(),
        wo=nc.dram_tensor("wo", [D, D], F32, kind="ExternalInput").ap(),
        bo=nc.dram_tensor("bo", [D], F32, kind="ExternalInput").ap(),
        x1=nc.dram_tensor("x1", [S, D], F32, kind="ExternalOutput").ap(),
        attnT=nc.dram_tensor("attnT", [D, S], BF16).ap(),
    )
    es = ExitStack()
    with es:
        consts = build_consts(nc, es)
        build_phase_a(nc, es, aps, consts)
    return nc


def _build_b():
    global _SEM_ES
    nc = bass.Bass("TRN2", target_bir_lowering=False)
    _SEM_ES = ExitStack()
    aps = dict(
        x1=nc.dram_tensor("x1", [S, D], F32, kind="ExternalInput").ap(),
        routers=nc.dram_tensor("routers", [4, D, 16], F32, kind="ExternalInput").ap(),
        w1=nc.dram_tensor("w1", [16, D, 4096], F32, kind="ExternalInput").ap(),
        w2=nc.dram_tensor("w2", [16, 4096, D], F32, kind="ExternalInput").ap(),
        x2=nc.dram_tensor("x2", [S, D], F32, kind="ExternalOutput").ap(),
        xtT0=nc.dram_tensor("xtT0", [D, S], BF16).ap(),
        xtT1=nc.dram_tensor("xtT1", [D, S], BF16).ap(),
        acc=nc.dram_tensor("acc", [S, D], F32).ap(),
    )
    es = ExitStack()
    with es:
        consts = build_consts(nc, es)
        build_phase_b(nc, es, aps, consts)
    return nc


def _build_c():
    global _SEM_ES
    nc = bass.Bass("TRN2", target_bir_lowering=False)
    _SEM_ES = ExitStack()
    aps = dict(
        x2=nc.dram_tensor("x2", [S, D], F32, kind="ExternalInput").ap(),
        wq=nc.dram_tensor("wq", [D, D], F32, kind="ExternalInput").ap(),
        bq=nc.dram_tensor("bq", [D], F32, kind="ExternalInput").ap(),
        wk=nc.dram_tensor("wk", [D, D], F32, kind="ExternalInput").ap(),
        bk=nc.dram_tensor("bk", [D], F32, kind="ExternalInput").ap(),
        wv=nc.dram_tensor("wv", [D, D], F32, kind="ExternalInput").ap(),
        bv=nc.dram_tensor("bv", [D], F32, kind="ExternalInput").ap(),
        wo=nc.dram_tensor("wo", [D, D], F32, kind="ExternalInput").ap(),
        bo=nc.dram_tensor("bo", [D], F32, kind="ExternalInput").ap(),
        x3=nc.dram_tensor("x3", [S, D], F32, kind="ExternalOutput").ap(),
        lnT=nc.dram_tensor("lnT", [D, S], BF16).ap(),
        qT=nc.dram_tensor("qT", [D, S], BF16).ap(),
        outT=nc.dram_tensor("outT", [D, S], BF16).ap(),
    )
    es = ExitStack()
    with es:
        consts = build_consts(nc, es)
        build_phase_c(nc, es, aps, consts)
    return nc


def _build_d():
    global _SEM_ES
    nc = bass.Bass("TRN2", target_bir_lowering=False)
    _SEM_ES = ExitStack()
    aps = dict(x3=nc.dram_tensor("x3", [S, D], F32, kind="ExternalInput").ap())
    shp = dict(q_w=[D, D], q_b=[D], k_w=[D, 512], k_b=[512], v_w=[D, 512],
               v_b=[512], o_w=[D, D], o_b=[D], router_w=[D, 8], router_b=[8],
               w1=[8, D, 2048], b1=[8, 2048], w2=[8, 2048, D], b2=[8, D])
    for k, v in shp.items():
        aps[k] = nc.dram_tensor(k, v, F32, kind="ExternalInput").ap()
    aps["x4"] = nc.dram_tensor("x4", [S, D], F32, kind="ExternalOutput").ap()
    aps["hsT"] = nc.dram_tensor("hsT", [D, S], BF16).ap()
    aps["hs"] = nc.dram_tensor("hs", [S, D], F32).ap()
    aps["qT"] = nc.dram_tensor("qT", [D, S], BF16).ap()
    aps["aoT"] = nc.dram_tensor("aoT", [512, S], BF16).ap()
    aps["mlaT"] = nc.dram_tensor("mlaT", [D, S], BF16).ap()
    aps["moe"] = nc.dram_tensor("moe", [S, D], F32).ap()
    es = ExitStack()
    with es:
        consts = build_consts(nc, es)
        build_phase_d(nc, es, aps, consts)
    return nc


def kernel(**inp):
    x0 = np.ascontiguousarray(inp["L0_attn_x"])
    B = x0.shape[0]

    nc = _build_a()
    in_maps = [dict(x0=x0[b], wq=inp["L0_attn_Wq"], wkv=inp["L0_attn_Wkv"],
                    wo=inp["L0_attn_Wo"], bo=inp["L0_attn_bo"]) for b in range(B)]
    res = run_bass_kernel_spmd(nc, in_maps, core_ids=list(range(B)))
    x1 = [np.asarray(res.results[b]["x1"]) for b in range(B)]

    nc = _build_b()
    in_maps = [dict(x1=x1[b], routers=inp["L0_ffn_routers"], w1=inp["L0_ffn_W1"],
                    w2=inp["L0_ffn_W2"]) for b in range(B)]
    res = run_bass_kernel_spmd(nc, in_maps, core_ids=list(range(B)))
    x2 = [np.asarray(res.results[b]["x2"]) for b in range(B)]

    nc = _build_c()
    in_maps = [dict(x2=x2[b], wq=inp["L1_attn_Wq"], bq=inp["L1_attn_bq"],
                    wk=inp["L1_attn_Wk"], bk=inp["L1_attn_bk"],
                    wv=inp["L1_attn_Wv"], bv=inp["L1_attn_bv"],
                    wo=inp["L1_attn_Wo"], bo=inp["L1_attn_bo"]) for b in range(B)]
    res = run_bass_kernel_spmd(nc, in_maps, core_ids=list(range(B)))
    x3 = [np.asarray(res.results[b]["x3"]) for b in range(B)]

    nc = _build_d()
    wd = {k: inp[f"L1_ffn_{k}"] for k in ["q_w", "q_b", "k_w", "k_b", "v_w", "v_b",
                                          "o_w", "o_b", "router_w", "router_b",
                                          "w1", "b1", "w2", "b2"]}
    in_maps = [dict(x3=x3[b], **wd) for b in range(B)]
    res = run_bass_kernel_spmd(nc, in_maps, core_ids=list(range(B)))
    x4 = [np.asarray(res.results[b]["x4"]) for b in range(B)]

    return np.stack(x4, axis=0).astype(np.float32)
